# version 35
# baseline (speedup 1.0000x reference)
import numpy as np
import concourse.bass as bass
import concourse.mybir as mybir
from concourse.bass_utils import run_bass_kernel_spmd
from contextlib import ExitStack

F32 = mybir.dt.float32
BF16 = mybir.dt.bfloat16
AF = mybir.ActivationFunctionType
ALU = mybir.AluOpType
AX = mybir.AxisListType

D = 1024
CH = 8
NT = 17
NTO = 16
NB = 24
TAPS = 40
VALID = 89
XNW = 2176
FW = 2048
EPS = 1e-5
ENGS = ("sync", "gpsimd", "tensor", "vector", "scalar")


def _hilo(t64):
    hi = t64.astype(np.float32)
    lo = (t64 - hi.astype(np.float64)).astype(np.float32)
    return hi, lo


def _make_tables():
    N = 128
    w = 2.0 * np.pi / N
    s = np.arange(N, dtype=np.float64)[:, None]
    acat = np.zeros((N, 128), dtype=np.float64)
    acat[:, 0:65] = np.cos(w * s * np.arange(0, 65)[None, :])
    acat[:, 65:128] = -np.sin(w * s * np.arange(1, 64)[None, :])
    t_i = np.arange(TAPS, dtype=np.float64)[:, None]
    e1 = np.zeros((128, 128), dtype=np.float64)
    e2 = np.zeros((128, 128), dtype=np.float64)
    e1[1:41, 0:65] = np.cos(w * t_i * np.arange(0, 65)[None, :])
    e1[1:41, 65:128] = -np.sin(w * t_i * np.arange(1, 64)[None, :])
    e2[1:41, 0:65] = -np.sin(w * t_i * np.arange(0, 65)[None, :])
    e2[1:41, 65:128] = np.cos(w * t_i * np.arange(1, 64)[None, :])
    u = np.arange(VALID, dtype=np.float64)[None, :] + (TAPS - 1)
    fcol = np.arange(0, 65, dtype=np.float64)[:, None]
    bre = np.cos(w * fcol * u) / N
    bre[1:64] *= 2.0
    bim = -2.0 * np.sin(w * fcol * u) / N
    bim[0] = 0.0
    bim[64] = 0.0
    bc1 = np.zeros((128, VALID), dtype=np.float64)
    bc2 = np.zeros((128, VALID), dtype=np.float64)
    bc1[0:65] = bre
    bc1[65:128] = -bre[1:64]
    bc2[0:65] = bim
    bc2[65:128] = bim[1:64]
    out = {}
    for nm, t in (("ta", acat), ("te1", e1), ("te2", e2), ("tb1", bc1), ("tb2", bc2)):
        hi, lo = _hilo(t)
        out[nm + "h"], out[nm + "l"] = hi, lo
    return out


class Prog:
    def __init__(self):
        self.steps = {e: [] for e in ENGS}

    def em(self, eng, fn):
        self.steps[eng].append(fn)


def build_nc():
    nc = bass.Bass("TRN2", target_bir_lowering=False)
    ap = {}
    ap["x"] = nc.dram_tensor("x", [2176, D], F32, kind="ExternalInput").ap()
    for nm in ("W1", "W2", "Wf", "A1", "A2"):
        ap[nm] = nc.dram_tensor(nm, [D, D], F32, kind="ExternalInput").ap()
    for nm in ("b1", "b2", "fb"):
        ap[nm] = nc.dram_tensor(nm, [D], F32, kind="ExternalInput").ap()
    ap["cst1"] = nc.dram_tensor("cst1", [128, 1419], F32, kind="ExternalInput").ap()
    ap["cst2"] = nc.dram_tensor("cst2", [128, 2081], F32, kind="ExternalInput").ap()
    ap["out"] = nc.dram_tensor("out", [2048, D], F32, kind="ExternalOutput").ap()
    ap["cci"] = nc.dram_tensor("cci", [128, 8], F32).ap()
    ap["cco"] = nc.dram_tensor("cco", [128, 8], F32).ap()

    es = ExitStack()
    T = {}

    def alloc(name, shape, dt, stack=None):
        T[name] = (stack or es).enter_context(nc.sbuf_tensor(name, shape, dt))
        return T[name]

    alloc("xn", [128, CH * XNW], BF16)
    for wb in ("WB0", "WB1", "WB2"):
        alloc(wb, [128, CH * 1024], BF16)
    alloc("identb", [128, 128], BF16)
    alloc("onesb", [128, 128], BF16)
    for nm in ("b1fm", "b2fm", "fbfm", "mw0fm", "ccsb", "totfm", "Kfm"):
        alloc(nm, [128, 8], F32)
    alloc("maskc", [128, 1], F32)
    alloc("onescol", [128, 1], BF16)
    alloc("ustat", [128, 16], F32)
    alloc("ub", [128, 8], BF16)
    alloc("ulb", [128, 8], BF16)
    alloc("sigs", [128, 2 * D], BF16)
    alloc("wfull", [128, D], BF16)
    alloc("tam", [128, 128], BF16)
    alloc("te1m", [128, 128], BF16)
    alloc("te2m", [128, 128], BF16)
    alloc("tb1m", [128, VALID], BF16)
    alloc("tb2m", [128, VALID], BF16)

    sems = {}
    semnames = (["GP", "CC", "T", "V", "SC", "CSTA", "CSTB", "CSTC", "GCC"]
                + [f"XT{j}" for j in range(5)] + [f"XG{j}" for j in range(5)]
                + ["SGA0", "SGA1", "SGA2", "SGA3", "GW0", "GW1", "GW2", "B1R",
                   "XE0", "XE1", "SO0", "SO1"])
    for s_ in semnames:
        sems[s_] = es.enter_context(nc.semaphore(s_))
    cnt = {s_: 0 for s_ in sems}

    def bump(sem, amt=1):
        cnt[sem] += amt
        return cnt[sem]

    def run_phase(prog):
        with nc.Block() as block:
            for eng in ENGS:
                steps = prog.steps[eng]
                if steps:
                    def body(e, _steps=steps):
                        for fn in _steps:
                            fn(e)
                    getattr(block, eng)(body)

    def W(e, sem, val=None):
        if val is None:
            sem, val = sem
        e.wait_ge(sems[sem], val)

    def I(instr, sem, amt=1):
        instr.then_inc(sems[sem], amt)

    def AW(instr, sem, val=None):
        if val is None:
            sem, val = sem
        instr._wait_ge(sems[sem], val)
        return instr

    alloc("P4", [128, CH * FW], BF16)
    pblk = [alloc("pblk0", [128, D], BF16)] * 2
    Xe = [alloc("Xe0", [128, D], BF16)] * 2
    pk1 = [alloc("pk1_0", [128, D], BF16)] * 2
    pk2 = [alloc("pk2_0", [128, D], BF16)] * 2
    SGg = [alloc("sgg0", [128, 2048], F32)] * 2
    PS6 = es.enter_context(nc.psum_tensor("PS6", [128, 3072], F32))

    def psP(a, b):
        return PS6[:, a:b]

    def psXs(a, b):
        return PS6[:, 1024 + a:1024 + b]

    def psFa(a, b):
        return PS6[:, 2048 + a:2048 + b]

    def psFb(a, b):
        return PS6[:, 2560 + a:2560 + b]

    pa = Prog()
    esA = ExitStack()
    xt = [esA.enter_context(nc.sbuf_tensor(f"xt{i}", [128, D], F32)) for i in range(4)]
    xnt = [esA.enter_context(nc.sbuf_tensor(f"xnt{i}", [128, D], BF16)) for i in range(3)]
    sqs = esA.enter_context(nc.sbuf_tensor("sqs", [128, D], F32))
    st_ = esA.enter_context(nc.sbuf_tensor("stats", [128, 64], F32))
    SG = [esA.enter_context(nc.sbuf_tensor(f"sg{i}", [128, 2048], F32)) for i in range(4)]
    cst1b = esA.enter_context(nc.sbuf_tensor("cst1b", [128, 1419], F32))
    RSb = esA.enter_context(nc.sbuf_tensor("RSb", [128, 312], F32))
    Ubm = esA.enter_context(nc.sbuf_tensor("Ubm", [128, 39], BF16))
    KBv = esA.enter_context(nc.sbuf_tensor("KBv", [128, 8], F32))
    bc0 = esA.enter_context(nc.sbuf_tensor("bc0", [128, 8], F32))
    cst2v = esA.enter_context(nc.sbuf_tensor("cst2v", [128, 33], F32))
    psT = [esA.enter_context(nc.psum_tensor("psT0", [128, D], BF16))] * 2
    psU = esA.enter_context(nc.psum_tensor("psU", [128, 512], F32))
    psT2 = PS6[:, 2048:2560].bitcast(BF16)
    psU2 = PS6[:, 2560:2568]

    def psT_of(k):
        return psT2 if (k < 8 and k % 2 == 1) else psT[0][:, :]

    def psU_of(k):
        return psU2 if (k < 8 and k % 2 == 0) else psU[:, 0:8]
    uev = {}

    s_w, s_x, v_wc, v_norm, v_evac, t_tr = {}, {}, {}, {}, {}, {}
    v_uacc = {}
    sc_sq_map = {}
    stats_v = {}

    sync_order = [("x", 8), ("w", "W1", 0), ("w", "W1", 1), ("x", 10)]
    sync_tail = [[("x", 12)], [("x", 14)], [("x", 16)], []]
    gpsimd_order = [("x", 1), ("x", 2), ("x", 3), ("x", 5), ("x", 6),
                    ("w", "W1", 2), ("x", 7), ("w", "W1", 3), ("x", 9),
                    ("x", 11), ("x", 13), ("x", 15)]
    W1SLOT = {0: 0, 1: 1, 2: 2, 3: 3}

    def emit_loads(eng, order):
        for item in order:
            if item[0] == "w":
                _, wnm, i = item

                def fs(e, _g=i, _w=wnm, _i=i):
                    instr = e.dma_start(
                        out=SG[_g][:, :],
                        in_=ap[_w][256 * _i:256 * (_i + 1), :].rearrange(
                            "(a p) n -> p a n", p=128))
                    I(instr, f"SGA{_g}", 16)
                pa.em(eng, fs)
                s_w[i] = (f"SGA{i}", bump(f"SGA{i}", 16))
            else:
                _, k = item
                sx = ("XT" if eng == "sync" else "XG") + str(k % 4)

                def fx(e, _k=k, _sx=sx):
                    if _k >= 4:
                        W(e, "SC", v_norm[_k - 4])
                    I(e.dma_start(out=xt[_k % 4][:, :],
                                  in_=ap["x"][128 * _k:128 * (_k + 1), :]),
                      _sx, 16)
                pa.em(eng, fx)
                s_x[k] = (sx, bump(sx, 16))

    def f_gmemset(e):
        I(e.memset(T["wfull"][:, :], 0.0), "GP", 1)
        I(e.memset(T["onescol"][:, :], 1.0), "GP", 1)
        I(e.memset(T["ustat"][:, 0:8], 0.0), "GP", 1)
    pa.em("gpsimd", f_gmemset)
    g_memset = bump("GP", 3)

    emit_loads("sync", [("x", 0)])

    def f_cst1(e):
        I(e.dma_start(out=cst1b[:, :], in_=ap["cst1"]), "CSTA", 16)
    pa.em("sync", f_cst1)
    csta1 = bump("CSTA", 16)

    emit_loads("sync", [("x", 4)])

    def f_cst2(e):
        I(e.dma_start(out=SGg[0][:, :], in_=ap["cst2"][:, 0:2048]), "CSTC", 16)
        I(e.dma_start(out=cst2v[:, :], in_=ap["cst2"][:, 2048:2081]), "CSTB", 16)
    pa.em("sync", f_cst2)
    cstc_mw = ("CSTC", bump("CSTC", 16))
    csta_final = ("CSTB", bump("CSTB", 16))

    emit_loads("sync", sync_order)
    emit_loads("gpsimd", gpsimd_order)

    C1OFF = {"ident": 0, "ones": 128, "tah": 256, "tal": 384, "te1h": 512,
             "te1l": 640, "te2h": 768, "te2l": 896, "tb1h": 1024, "tb1l": 1113,
             "tb2h": 1202, "tb2l": 1291, "ub": 1380}

    def c1(nm, w=128):
        o = C1OFF[nm]
        return cst1b[:, o:o + w]

    def fvid_emit():
      def fvid(e, _pre=cnt["V"]):
        W(e, "CSTA", csta1)
        AW(e.tensor_copy(T["identb"][:, :], c1("ident")), "V", _pre)
        for mrg, hi, lo in (("tam", "tah", "tal"), ("te1m", "te1h", "te1l"),
                            ("te2m", "te2h", "te2l")):
            e.tensor_tensor(T[mrg][:, :], c1(hi), c1(lo), op=ALU.add)
        for mrg, hi, lo in (("tb1m", "tb1h", "tb1l"), ("tb2m", "tb2h", "tb2l")):
            e.tensor_tensor(T[mrg][:, :], c1(hi, VALID), c1(lo, VALID), op=ALU.add)
        e.tensor_copy(Ubm[:, :], c1("ub", 39))
        I(e.tensor_copy(T["onesb"][:, :], c1("ones")), "V", 1)
      pa.em("vector", fvid)
      fm["v_identb"] = bump("V")

    def fvm_emit():
      def fvm(e, _pre=cnt["V"]):
        W(e, *csta_final)
        W(e, *cstc_mw)
        i = e.tensor_copy(T["b1fm"][:, :], cst2v[:, 0:8])
        AW(i, "V", _pre)
        e.tensor_copy(T["b2fm"][:, :], cst2v[:, 8:16])
        e.tensor_copy(T["fbfm"][:, :], cst2v[:, 16:24])
        e.tensor_copy(T["mw0fm"][:, :], cst2v[:, 24:32])
        i = e.tensor_copy(T["maskc"][:, :], cst2v[:, 32:33])
        I(i, "V", 1)
        i = e.tensor_tensor(T["wfull"][0:41, :], SGg[0][0:41, 0:1024],
                            SGg[0][0:41, 1024:2048], op=ALU.subtract)
        AW(i, "GP", g_memset)
        I(i, "V", 1)
      pa.em("vector", fvm)
      fm["v_fmv"] = bump("V")
      fm["v_wfull"] = bump("V")

    fm = {}

    def emit_rs(k):
        def fs1(e, _k=k, _pre=cnt["SC"]):
            W(e, *s_x[_k])
            i = e.activation(sqs[:, :], xt[_k % 4][:, :], AF.Identity,
                             accum_out=st_[:, _k % 4:_k % 4 + 1])
            AW(i, "SC", _pre)
            I(i, "SC", 1)
            i = e.activation(sqs[:, :], xt[_k % 4][:, :], AF.Square,
                             accum_out=st_[:, 4 + _k % 4:5 + _k % 4])
            AW(i, "SC", _pre + 1)
            I(i, "SC", 1)
        pa.em("scalar", fs1)
        bump("SC")
        sc_sq_map[k] = bump("SC")

    def emit_stats(tiles):
        nj = len(tiles)

        def fv2g(e, _sc=sc_sq_map[tiles[-1]], _v0=cnt["V"], _nj=nj):
            W(e, "SC", _sc)
            v = _v0
            i = e.tensor_scalar(st_[:, 8:8 + _nj], st_[:, 0:_nj], 1.0 / D, None,
                                op0=ALU.mult)
            AW(i, "V", v); I(i, "V", 1); v += 1
            i = e.tensor_tensor(st_[:, 12:12 + _nj], st_[:, 8:8 + _nj],
                                st_[:, 8:8 + _nj], op=ALU.mult)
            AW(i, "V", v); I(i, "V", 1); v += 1
            i = e.scalar_tensor_tensor(st_[:, 16:16 + _nj], st_[:, 4:4 + _nj],
                                       1.0 / D, st_[:, 12:12 + _nj],
                                       op0=ALU.mult, op1=ALU.subtract)
            AW(i, "V", v); I(i, "V", 1); v += 1
            i = e.tensor_scalar(st_[:, 16:16 + _nj], st_[:, 16:16 + _nj], EPS, None,
                                op0=ALU.add)
            AW(i, "V", v); I(i, "V", 1); v += 1
            i = e.reciprocal(st_[:, 20:20 + _nj], st_[:, 16:16 + _nj])
            AW(i, "V", v); I(i, "V", 1); v += 1
        pa.em("vector", fv2g)
        v_rec = bump("V", 5)

        def fs2g(e, _v=v_rec, _pre=cnt["SC"], _nj=nj):
            W(e, "V", _v)
            i = e.activation(st_[:, 24:24 + _nj], st_[:, 20:20 + _nj], AF.Sqrt)
            AW(i, "SC", _pre)
            I(i, "SC", 1)
        pa.em("scalar", fs2g)
        sc_sqrt = bump("SC")

        def fv3g(e, _sc=sc_sqrt, _v0=cnt["V"], _nj=nj):
            W(e, "SC", _sc)
            v = _v0
            i = e.tensor_tensor(st_[:, 28:28 + _nj], st_[:, 24:24 + _nj],
                                st_[:, 24:24 + _nj], op=ALU.mult)
            AW(i, "V", v); I(i, "V", 1); v += 1
            i = e.tensor_tensor(st_[:, 28:28 + _nj], st_[:, 28:28 + _nj],
                                st_[:, 16:16 + _nj], op=ALU.mult)
            AW(i, "V", v); I(i, "V", 1); v += 1
            i = e.tensor_scalar(st_[:, 28:28 + _nj], st_[:, 28:28 + _nj], -0.5,
                                1.5, op0=ALU.mult, op1=ALU.add)
            AW(i, "V", v); I(i, "V", 1); v += 1
            i = e.tensor_tensor(st_[:, 32:32 + _nj], st_[:, 24:24 + _nj],
                                st_[:, 28:28 + _nj], op=ALU.mult)
            AW(i, "V", v); I(i, "V", 1); v += 1
            i = e.scalar_tensor_tensor(st_[:, 36:36 + _nj], st_[:, 8:8 + _nj],
                                       -1.0, st_[:, 32:32 + _nj],
                                       op0=ALU.mult, op1=ALU.mult)
            AW(i, "V", v); I(i, "V", 1); v += 1
        pa.em("vector", fv3g)
        stats_v[tiles[0] // 4] = bump("V", 5)

    def emit_nm(k):
        def fnorm(e, _k=k, _pre=cnt["SC"]):
            W(e, "V", stats_v[_k // 4])
            if _k >= 3:
                W(e, "T", t_tr[_k - 3])
            j = _k % 4
            i = e.activation(xnt[_k % 3][:, :], xt[_k % 4][:, :], AF.Identity,
                             scale=st_[:, 32 + j:33 + j],
                             bias=st_[:, 36 + j:37 + j])
            AW(i, "SC", _pre)
            I(i, "SC", 1)
        pa.em("scalar", fnorm)
        v_norm[k] = bump("SC")

    def emit_tf(k):
        def ft(e, _k=k, _pre=cnt["T"]):
            W(e, "SC", v_norm[_k])
            if _k == 0:
                W(e, "V", fm["v_identb"])
            if _k == 1:
                W(e, "GP", g_memset)
            pT = _k - 2 if _k <= 8 else _k - 1
            if pT >= 0:
                W(e, "V", v_evac[pT])
            pU = _k - 2 if _k < 8 else _k - 1
            if pU >= 1:
                W(e, "V", v_uacc[pU])
            dstT = psT_of(_k)
            first = True
            for c in range(CH):
                instr = e.transpose(dstT[:, 128 * c:128 * (c + 1)],
                                    xnt[_k % 3][:, 128 * c:128 * (c + 1)],
                                    T["identb"][:, :])
                if first:
                    AW(instr, "T", _pre)
                    first = False
            if _k >= 1:
                dstU = psU_of(_k)
                for c in range(CH):
                    instr = e.matmul(dstU[:, c:c + 1],
                                     xnt[_k % 3][:, 128 * c:128 * (c + 1)],
                                     T["onescol"][:, :],
                                     start=True, stop=True)
            I(instr, "T", 1)
        pa.em("tensor", ft)
        t_tr[k] = bump("T")

        def fve(e, _k=k, _pre=cnt["V"]):
            W(e, "T", t_tr[_k])
            xn3 = T["xn"][:, 0:CH * XNW].rearrange("p (c w) -> p c w", c=CH)
            ps3 = psT_of(_k).rearrange("p (c w) -> p c w", c=CH)
            instr = e.tensor_copy(xn3[:, :, 128 * _k:128 * (_k + 1)], ps3)
            AW(instr, "V", _pre)
            I(instr, "V", 1)
        pa.em("vector", fve)
        v_evac[k] = bump("V")

        if k >= 1:
            def fuacc(e, _k=k, _pre=cnt["V"]):
                if _k == 1:
                    W(e, "GP", g_memset)
                instr = e.tensor_tensor(T["ustat"][:, 0:8], T["ustat"][:, 0:8],
                                        psU_of(_k), op=ALU.add)
                AW(instr, "V", _pre)
                I(instr, "V", 1)
            pa.em("vector", fuacc)
            v_uacc[k] = bump("V")

    def emit_wc_w1(pc):
        def fvw(e, _pc=pc, _pre=cnt["V"]):
            W(e, *s_w[_pc])
            i = e.tensor_copy(T["WB0"][:, 2048 * _pc:2048 * (_pc + 1)],
                              SG[_pc][:, :])
            AW(i, "V", _pre)
            I(i, "V", 1)
        pa.em("vector", fvw)
        v_wc[pc] = bump("V")

    emit_rs(0)
    fvid_emit()
    for k in (1, 2, 3):
        emit_rs(k)
    emit_stats([0, 1, 2, 3])
    for k in (0, 1, 2, 3):
        emit_nm(k)
        emit_tf(k)
    fvm_emit()
    for k in (4, 5, 6, 7):
        emit_rs(k)
    emit_stats([4, 5, 6, 7])
    for k in (4, 5, 6, 7):
        emit_nm(k)
        emit_tf(k)
    for pc in range(4):
        emit_wc_w1(pc)

    def ftc0(e, _pre=cnt["T"]):
        W(e, "V", fm["v_wfull"])
        W(e, "GP", g_memset)
        first = True
        for half in range(2):
            i = e.matmul(psXs(512 * half, 512 * (half + 1)), T["te1m"][:, :],
                         T["wfull"][:, 512 * half:512 * (half + 1)], start=True,
                         stop=True)
            if first:
                AW(i, "T", _pre)
                first = False
        for half in range(2):
            instr = e.matmul(psP(512 * half, 512 * (half + 1)), T["te2m"][:, :],
                             T["wfull"][:, 512 * half:512 * (half + 1)], start=True,
                             stop=True)
        I(instr, "T", 1)
    pa.em("tensor", ftc0)
    t_sig = bump("T")

    def fvc1(e, _pre=cnt["V"]):
        W(e, "T", t_sig)
        i = e.tensor_copy(T["sigs"][:, 0:D], psXs(0, 1024))
        AW(i, "V", _pre)
        I(e.tensor_copy(T["sigs"][:, D:2 * D], psP(0, 1024)), "V", 1)
    pa.em("vector", fvc1)
    v_sig = bump("V")

    def ftS(e, _pre=cnt["T"]):
        W(e, "V", v_sig)
        first = True
        for c in range(CH):
            i = e.matmul(psXs(39 * c, 39 * c + 39),
                         T["wfull"][:, 128 * c:128 * (c + 1)], Ubm[:, :],
                         start=True, stop=True)
            if first:
                AW(i, "T", _pre)
                first = False
        for c in range(CH):
            instr = e.matmul(psXs(960 + c, 961 + c),
                             T["wfull"][:, 128 * c:128 * (c + 1)],
                             T["onescol"][:, :], start=True, stop=True)
        I(instr, "T", 1)
    pa.em("tensor", ftS)
    t_S = bump("T")

    def fvS(e, _v0=cnt["V"]):
        W(e, "T", t_S)
        v = _v0
        i = e.tensor_copy(RSb[:, :], psXs(0, 312))
        AW(i, "V", v); I(i, "V", 1); v += 1
        i = e.tensor_tensor(KBv[:, :], T["b1fm"][:, :], psXs(960, 968),
                            op=ALU.mult)
        AW(i, "V", v); I(i, "V", 1); v += 1
        i = e.tensor_scalar(bc0[:, :], T["b1fm"][:, :], T["maskc"][:, 0:1],
                            -1.0, op0=ALU.mult, op1=ALU.mult)
        AW(i, "V", v); I(i, "V", 1); v += 1
    pa.em("vector", fvS)
    v_S = bump("V", 3)

    g_wp, v_wcG = {}, {}
    WPIECES = [("Wf", i) for i in range(4)] + [("A1", i) for i in range(4)] \
        + [("A2", i) for i in range(4)] + [("W2", i) for i in range(4)]
    WDEST = {"Wf": "WB1", "A1": "WB2", "A2": "WB0", "W2": "WB1"}

    def wp_buf(gi):
        return SG[0] if (gi < 8 and gi % 2 == 1) else SGg[0]

    def emit_wpiece(gi, prog):
        wnm, i = WPIECES[gi]

        def fgw(e, _g=gi, _w=wnm, _i=i):
            if _g == 0:
                W(e, "V", fm["v_wfull"])
            elif _g == 1:
                W(e, "V", v_wc[0])
            elif _g < 8:
                W(e, "V", v_wcG[_g - 2])
            else:
                W(e, "V", v_wcG[_g - 1])
            _sem = "GW2" if (_g < 8 and _g % 2 == 1) else "GW0"
            I(e.dma_start(out=wp_buf(_g)[:, :],
                          in_=ap[_w][256 * _i:256 * (_i + 1), :].rearrange(
                              "(a p) n -> p a n", p=128)), _sem, 16)
        prog.em("sync", fgw)
        sem = "GW2" if (gi < 8 and gi % 2 == 1) else "GW0"
        g_wp[gi] = (sem, bump(sem, 16))

    def emit_wcast(gi, prog):
        wdst = WDEST[WPIECES[gi][0]]
        po = WPIECES[gi][1]

        def fvwc(e, _g=gi, _wd=wdst, _po=po, _pre=cnt["V"]):
            W(e, *g_wp[_g])
            inst = e.tensor_copy(T[_wd][:, 2048 * _po:2048 * (_po + 1)],
                                 wp_buf(_g)[:, :])
            AW(inst, "V", _pre)
            I(inst, "V", 1)
        prog.em("vector", fvwc)
        v_wcG[gi] = bump("V")

    for gi in range(8):
        emit_wpiece(gi, pa)
        emit_loads("sync", sync_tail[gi]) if gi < 4 else None

    def emit_u_tail():
        def fvut(e, _v0=cnt["V"]):
            v = _v0
            i = e.tensor_copy(T["ub"][:, :], T["ustat"][:, 0:8])
            AW(i, "V", v); I(i, "V", 1); v += 1
            i = e.tensor_tensor(T["ustat"][:, 8:16], T["ustat"][:, 0:8],
                                T["ub"][:, :], op=ALU.subtract)
            AW(i, "V", v); I(i, "V", 1); v += 1
            i = e.tensor_copy(T["ulb"][:, :], T["ustat"][:, 8:16])
            AW(i, "V", v); I(i, "V", 1); v += 1
        pa.em("vector", fvut)
        uev["v_u"] = bump("V", 3)

        def ftu(e, _pre=cnt["T"]):
            W(e, "V", uev["v_u"])
            first = True
            for ec in range(CH):
                for dk in range(CH):
                    stat = T["WB0"][:, dk * 1024 + ec * 128: dk * 1024 + (ec + 1) * 128]
                    i = e.matmul(psU[:, ec:ec + 1], stat,
                                 T["ub"][:, dk:dk + 1], start=(dk == 0), stop=False)
                    if first:
                        AW(i, "T", _pre)
                        first = False
                    instr = e.matmul(psU[:, ec:ec + 1], stat,
                                     T["ulb"][:, dk:dk + 1], start=False,
                                     stop=(dk == CH - 1))
            I(instr, "T", 1)
        pa.em("tensor", ftu)
        uev["t_u"] = bump("T")

        def fvp(e, _pre=cnt["V"]):
            W(e, "T", uev["t_u"])
            i = e.scalar_tensor_tensor(T["ccsb"][:, :], T["b1fm"][:, :], 2048.0,
                                       psU[:, 0:8], op0=ALU.mult, op1=ALU.add)
            AW(i, "V", _pre)
            I(i, "V", 1)
        pa.em("vector", fvp)
        uev["v_partial"] = bump("V")

    t_p, t_X, t_f, sc_p, sc_X, v_pk, v_fe = {}, {}, {}, {}, {}, {}, {}

    def tmax_of(i):
        return min(NT - 1, (VALID * (i + 1) + 127) // 128) if i < NB - 1 else NT - 1

    def emit_conv_te(i):
        if i <= NB - 1:
            assert tmax_of(i) in v_evac, f"conv block {i} needs un-emitted tile"

            def ftp(e, _i=i, _pre=cnt["T"]):
                if _i == 0:
                    W(e, "V", v_sig)
                    W(e, "V", v_wc[3])
                else:
                    W(e, "SC", sc_p[_i - 1])
                W(e, "V", v_evac[tmax_of(_i)])
                so = VALID * (_i + 1) if _i < NB - 1 else 2048
                first = True
                for dk in range(CH):
                    for half in range(2):
                        instr = e.matmul(
                            psP(512 * half, 512 * (half + 1)),
                            T["xn"][:, dk * XNW + so: dk * XNW + so + 128],
                            T["WB0"][:, dk * 1024 + 512 * half: dk * 1024 + 512 * (half + 1)],
                            start=(dk == 0), stop=(dk == CH - 1))
                        if first:
                            AW(instr, "T", _pre)
                            first = False
                I(instr, "T", 1)
            pa.em("tensor", ftp)
            t_p[i] = bump("T")
        if 1 <= i <= NB:
            b = i - 1

            def ftx(e, _b=b, _pre=cnt["T"]):
                if _b == 0:
                    W(e, "SC", sc_p[0])
                    W(e, "V", v_S)
                else:
                    W(e, "SC", sc_p[_b])
                    W(e, "V", v_pk[_b - 1])
                first = True
                for half in range(2):
                    instr = e.matmul(psXs(512 * half, 512 * (half + 1)),
                                     T["tam"][:, :],
                                     pblk[_b % 2][:, 512 * half:512 * (half + 1)],
                                     start=True, stop=True)
                    if first:
                        AW(instr, "T", _pre)
                        first = False
                I(instr, "T", 1)
            pa.em("tensor", ftx)
            t_X[b] = bump("T")
        if 2 <= i <= NB + 1:
            b2 = i - 2

            def ftf(e, _b=b2, _pre=cnt["T"]):
                W(e, "V", v_pk[_b])
                if _b >= 1:
                    W(e, "V", v_fe[_b - 1])
                first = True
                for dk in range(CH):
                    pf = (psFa(89 * dk, 89 * dk + 89) if dk < 5
                          else psFb(89 * (dk - 5), 89 * (dk - 5) + 89))
                    inst = e.matmul(pf, pk1[_b % 2][:, dk * 128:(dk + 1) * 128],
                                    T["tb1m"][:, :], start=True, stop=False)
                    if first:
                        AW(inst, "T", _pre)
                        first = False
                    instr = e.matmul(pf, pk2[_b % 2][:, dk * 128:(dk + 1) * 128],
                                     T["tb2m"][:, :], start=False, stop=True)
                I(instr, "T", 1)
            pa.em("tensor", ftf)
            t_f[b2] = bump("T")

    def emit_conv_sc(i):
        if i <= NB - 1:
            def fsp(e, _i=i, _pre=cnt["SC"]):
                W(e, "T", t_p[_i])
                if _i >= 1:
                    W(e, "T", t_X[_i - 1])
                inst = e.copy(pblk[_i % 2][:, :], psP(0, 1024))
                AW(inst, "SC", _pre)
                I(inst, "SC", 1)
            pa.em("scalar", fsp)
            sc_p[i] = bump("SC")


    def emit_conv_v(i):
        if 2 <= i <= NB + 1:
            b2 = i - 2

            def fvfe(e, _b=b2, _pre=cnt["V"]):
                W(e, "T", t_f[_b])
                ln = VALID
                wo = VALID * _b if _b < NB - 1 else FW - VALID
                first = True
                for dk in range(CH):
                    pf = (psFa(89 * dk, 89 * dk + ln) if dk < 5
                          else psFb(89 * (dk - 5), 89 * (dk - 5) + ln))
                    instr = e.tensor_copy(
                        T["P4"][:, dk * FW + wo: dk * FW + wo + ln], pf)
                    if first:
                        AW(instr, "V", _pre)
                        first = False
                I(instr, "V", 1)
            pa.em("vector", fvfe)
            v_fe[b2] = bump("V")
        if 1 <= i <= NB:
            b = i - 1

            def fvpk(e, _b=b, _pre=cnt["V"]):
                W(e, "T", t_X[_b])
                if _b >= 1:
                    W(e, "T", t_f[_b - 1])
                inst = e.tensor_tensor(pk1[_b % 2][:, :], psXs(0, 1024),
                                       T["sigs"][:, 0:D], op=ALU.mult)
                AW(inst, "V", _pre)
                I(e.tensor_tensor(pk2[_b % 2][:, :], psXs(0, 1024),
                                  T["sigs"][:, D:2 * D], op=ALU.mult), "V", 1)
            pa.em("vector", fvpk)
            v_pk[b] = bump("V")

    units = []
    for g in (2, 3):
        a = 4 * g
        tiles = list(range(a, a + 4))
        for k in tiles:
            units.append(("rs", k))
        units.append(("st", tiles))
        units += [("nm", a), ("nm", a + 1), ("tf", a), ("nm", a + 2),
                  ("tf", a + 1), ("nm", a + 3), ("tf", a + 2), ("tf", a + 3)]
    units += [("rs", 16), ("st", [16]), ("nm", 16), ("tf", 16)]
    unit_sched = {it: [2 * it, 2 * it + 1] for it in range(15)}
    wcC_sched = {6: 0, 8: 1, 10: 2, 12: 3, 14: 4, 16: 5, 18: 6, 20: 7}

    for i in range(NB + 3):
        emit_conv_te(i)
        emit_conv_sc(i)
        emit_conv_v(i)
        for uj in unit_sched.get(i, []):
            kind, arg = units[uj]
            if kind == "rs":
                emit_rs(arg)
            elif kind == "st":
                emit_stats(arg)
            elif kind == "nm":
                emit_nm(arg)
            else:
                emit_tf(arg)
        if i == 16:
            emit_u_tail()
        if i == 3:
            def fcorr(e, _pre=cnt["V"]):
                W(e, "V", v_fe[0])
                first = True
                for c in range(CH):
                    ins = e.scalar_tensor_tensor(
                        T["P4"][:, c * FW:c * FW + 39], RSb[:, 39 * c:39 * c + 39],
                        bc0[:, c:c + 1], T["P4"][:, c * FW:c * FW + 39],
                        op0=ALU.mult, op1=ALU.add)
                    if first:
                        AW(ins, "V", _pre)
                        first = False
                I(ins, "V", 1)
            pa.em("vector", fcorr)
            v_corr = bump("V")
        if i in wcC_sched:
            emit_wcast(wcC_sched[i], pa)

    def fgcc(e):
        i = e.dma_start(out=ap["cci"], in_=T["ccsb"][:, :])
        AW(i, "V", uev["v_partial"])
        I(i, "GCC", 16)
        i = e.collective_compute("AllReduce", ALU.add,
                                 replica_groups=[list(range(8))],
                                 ins=[ap["cci"].opt()], outs=[ap["cco"].opt()])
        AW(i, "GCC", g_ccin_v)
        I(i, "CC", 1)
        i = e.dma_start(out=T["totfm"][:, :], in_=ap["cco"])
        AW(i, "CC", cc_done_v)
        I(i, "GCC", 16)
    g_ccin_v = cnt["GCC"] + 16
    cc_done_v = cnt["CC"] + 1
    pa.em("gpsimd", fgcc)
    bump("GCC", 16)
    bump("CC", 1)
    g_tot = ("GCC", bump("GCC", 16))

    run_phase(pa)
    esA.close()
    alloc("P2", [128, CH * FW], BF16)
    alloc("P3", [128, CH * FW], BF16)
    b2bc = T["sigs"][:, :].bitcast(F32)
    pm = Prog()
    esM = ExitStack()
    psH2 = esM.enter_context(nc.psum_tensor("psH2", [128, 1024], F32))

    def psH(i, a, b):
        if i == 0:
            return PS6[:, a:b]
        if b <= 1024:
            return PS6[:, 2048 + a:2048 + b]
        assert a >= 1024
        return psH2[:, a - 1024:b - 1024]

    def psH4(i, a, b):
        assert b <= 1024
        if i < 3:
            return PS6[:, 1024 * i + a:1024 * i + b]
        return psH2[:, a:b]

    xt2 = [SGg[0][:, 0:1024], SGg[0][:, 1024:2048]]
    tt512 = SGg[0][:, 1024:1536]

    def dense_tm(name, wbuf, src, srcw, dst, evac_kind, tw0=(), tw1=(), sw0=(),
                 bias_ready=None, per_tile=None, badd=None):
        t_h, ev, v_add = {}, {}, {}
        bias_mm = False
        scr = [SGg[0][:, 0:1024], SGg[0][:, 1024:2048]]
        for tk in range(NTO):
            def fth(e, _t=tk, _pre=cnt["T"]):
                if _t == 0:
                    for wv in tw0:
                        W(e, *wv)
                    if bias_mm and bias_ready is not None:
                        W(e, "V", uev[bias_ready])
                if _t == 1:
                    for wv in tw1:
                        W(e, *wv)
                if _t < 4 and evac_kind != "tanh_silu":
                    W(e, "V", uev["v_add_h3"][NTO - 4 + _t])
                if _t >= 4:
                    if evac_kind == "tanh_silu":
                        W(e, "V", v_add[_t - 4])
                    else:
                        W(e, "V", ev[_t - 4])
                first = True
                for dk in range(CH):
                    for half in range(2):
                        instr = e.matmul(
                            psH4(_t % 4, 512 * half, 512 * (half + 1)),
                            T[src][:, dk * srcw + 128 * _t: dk * srcw + 128 * (_t + 1)],
                            T[wbuf][:, dk * 1024 + 512 * half: dk * 1024 + 512 * (half + 1)],
                            start=(dk == 0), stop=(False if bias_mm else dk == CH - 1))
                        if first:
                            AW(instr, "T", _pre)
                            first = False
                I(instr, "T", 1)
            pm.em("tensor", fth)
            t_h[tk] = bump("T")
            if evac_kind == "tanh_silu":
                def fva(e, _t=tk, _pre=cnt["V"]):
                    W(e, "T", t_h[_t])
                    if _t == 0:
                        for wv in sw0:
                            W(e, *wv)
                        W(e, *uev["g_fb"])
                    if _t >= 2:
                        W(e, "SC", ev[_t - 2])
                    i = e.tensor_tensor(scr[_t % 2][:, 0:1024],
                                        psH4(_t % 4, 0, 1024), b2bc, op=ALU.add)
                    AW(i, "V", _pre)
                    I(i, "V", 1)
                pm.em("vector", fva)
                v_add[tk] = bump("V")

                def fsh(e, _t=tk, _s0=cnt["SC"]):
                    W(e, "V", v_add[_t])
                    st = _s0
                    for sp in range(2):
                        sc_ = scr[_t % 2][:, 512 * sp:512 * (sp + 1)]
                        i = e.activation(sc_, sc_, AF.Tanh)
                        AW(i, "SC", st); I(i, "SC", 1); st += 1
                        i = e.activation(
                            T[dst][:, 1024 * _t + 512 * sp: 1024 * _t + 512 * (sp + 1)],
                            sc_, AF.Silu)
                        AW(i, "SC", st); I(i, "SC", 1); st += 1
                pm.em("scalar", fsh)
                ev[tk] = bump("SC", 4)
            else:
                def fvh(e, _t=tk, _pre=cnt["V"]):
                    W(e, "T", t_h[_t])
                    if badd is not None and _t == 0:
                        W(e, *uev[badd])
                    if badd is None:
                        i = e.tensor_copy(T[dst][:, 1024 * _t:1024 * (_t + 1)],
                                          psH4(_t % 4, 0, 1024))
                        AW(i, "V", _pre)
                    else:
                        i = e.tensor_tensor(T[dst][:, 1024 * _t:1024 * (_t + 1)],
                                            psH4(_t % 4, 0, 1024), b2bc,
                                            op=ALU.add)
                        AW(i, "V", _pre)
                    I(i, "V", 1)
                pm.em("vector", fvh)
                ev[tk] = bump("V")
            if per_tile is not None:
                per_tile(tk)
        uev["v_add_" + name] = dict(v_add)
        return t_h, ev

    def emit_vrow_cast(stage_cols, free_after, dma_evt):
        def fvr(e, _pre=cnt["V"], _c=stage_cols, _fa=free_after, _de=dma_evt):
            W(e, *uev[_de])
            W(e, "T", _fa)
            i = e.tensor_copy(T["b1row"][0:1, :], SGg[0][0:1, _c:_c + 1024])
            AW(i, "V", _pre)
            I(i, "V", 1)
        pm.em("vector", fvr)
        return bump("V")

    def dense_pass(name, wbuf, src, srcw, srco, dst, evac_kind,
                   tw0=(), tw1=(), vw0=(), sw0=(), evw=(), wload=None):
        t_h, v_he, sc_free = {}, {}, {}
        wcB_sched = {1: 0, 3: 1, 5: 2, 7: 3}
        if wload is not None:
            for gi in wload:
                emit_wpiece(gi, pm)
        for ec in range(CH):
            def fth(e, _e=ec, _pre=cnt["T"]):
                if _e == 0:
                    for wv in tw0:
                        W(e, *wv)
                if _e == 1:
                    for wv in tw1:
                        W(e, *wv)
                if _e >= 2:
                    if evac_kind == "tanh_silu":
                        W(e, "SC", sc_free[_e - 2])
                    else:
                        W(e, "V", v_he[_e - 2])
                first = True
                for dk in range(CH):
                    for sp in range(4):
                        instr = e.matmul(
                            psH(_e % 2, 512 * sp, 512 * (sp + 1)),
                            T[wbuf][:, dk * 1024 + _e * 128: dk * 1024 + (_e + 1) * 128],
                            T[src][:, dk * srcw + srco + 512 * sp:
                                   dk * srcw + srco + 512 * (sp + 1)],
                            start=(dk == 0), stop=(dk == CH - 1))
                        if first:
                            AW(instr, "T", _pre)
                            first = False
                I(instr, "T", 1)
            pm.em("tensor", fth)
            t_h[ec] = bump("T")
            if evac_kind in ("copy", "bias_b2"):
                def fvh(e, _e=ec, _pre=cnt["V"]):
                    W(e, "T", t_h[_e])
                    if _e == 0:
                        for wv in evw:
                            W(e, *wv)
                    i = e.tensor_copy(T[dst][:, _e * FW:_e * FW + 1024],
                                      psH(_e % 2, 0, 1024))
                    AW(i, "V", _pre)
                    i = e.tensor_copy(T[dst][:, _e * FW + 1024:(_e + 1) * FW],
                                      psH(_e % 2, 1024, 2048))
                    I(i, "V", 1)
                pm.em("vector", fvh)
                v_he[ec] = bump("V")
            else:
                def fsh(e, _e=ec, _s0=cnt["SC"]):
                    W(e, "T", t_h[_e])
                    if _e == 0:
                        for wv in sw0:
                            W(e, *wv)
                    s = _s0
                    for sp in range(4):
                        i = e.activation(tt512, psH(_e % 2, 512 * sp, 512 * (sp + 1)),
                                         AF.Tanh, bias=T["fbfm"][:, _e:_e + 1],
                                         scale=1.0)
                        AW(i, "SC", s); I(i, "SC", 1); s += 1
                        i = e.activation(
                            T[dst][:, _e * FW + 512 * sp: _e * FW + 512 * (sp + 1)],
                            tt512, AF.Silu)
                        AW(i, "SC", s); I(i, "SC", 1); s += 1
                pm.em("scalar", fsh)
                sc_free[ec] = bump("SC", 8)
            if wload is not None and ec in wcB_sched:
                gidx = wcB_sched[ec]
                if gidx < len(wload):
                    emit_wcast(wload[gidx], pm)
        return t_h, v_he, sc_free

    def emit_fvK():
      def fvK(e, _pre=cnt["V"]):
        W(e, *g_tot)
        inst = e.tensor_tensor(T["Kfm"][:, :], T["mw0fm"][:, :],
                               T["totfm"][:, :], op=ALU.mult)
        AW(inst, "V", _pre)
        I(inst, "V", 1)
        inst = e.tensor_tensor(T["Kfm"][:, :], T["Kfm"][:, :], KBv[:, :],
                               op=ALU.add)
        AW(inst, "V", _pre + 1)
        I(inst, "V", 1)
      pm.em("vector", fvK)
      bump("V", 2)

    v_kadd = {}

    def emit_kadd(rng):
      for ec in rng:
        def fka(e, _e=ec, _pre=cnt["V"]):
            if _e == 0:
                W(e, "T", t_f[NB - 1])
            i = e.tensor_scalar(T["P4"][:, _e * FW:(_e + 1) * FW],
                                T["P4"][:, _e * FW:(_e + 1) * FW],
                                T["Kfm"][:, _e:_e + 1], None, op0=ALU.add)
            AW(i, "V", max(_pre, v_fe.get(NB - 1, 0)))
            I(i, "V", 1)
        pm.em("vector", fka)
        v_kadd[ec] = bump("V")


    t_h1, v_he1, _ = dense_pass(
        "h1", "WB1", "xn", XNW, 128, "P2", "copy",
        tw0=[("SC", sc_p[NB - 1]), ("V", v_pk[NB - 1]), ("V", v_wcG[3])],
        tw1=[("V", v_fe[NB - 1])],
        wload=[8, 9, 10, 11])
    t_h2, v_he2, _ = dense_pass(
        "h2", "WB2", "P2", FW, 0, "xn", "copy",
        tw0=[("V", v_he1[CH - 1]), ("V", v_wcG[7])],
        evw=[("T", t_h1[CH - 1])],
        wload=[12, 13, 14, 15])
    emit_fvK()
    emit_kadd(range(0, 8))
    def f_xcopy(e):
        I(e.dma_start(out=ap["out"], in_=ap["x"][128:2176, :]), "B1R", 16)
    pm.em("gpsimd", f_xcopy)
    g_xcopy = bump("B1R", 16)

    def fgrow(e):
        W(e, "V", v_pk[NB - 1])
        i = e.dma_start(out=b2bc.rearrange("p (o n) -> p o n", o=1),
                        in_=ap["fb"].rearrange("(o n) -> o n", o=1)
                        .partition_broadcast(128))
        I(i, "GW1", 16)
    pm.em("gpsimd", fgrow)
    uev["g_fb"] = ("GW1", bump("GW1", 16))

    def fgrow2(e):
        W(e, "V", uev["v_add_h3"][NTO - 1])
        i = e.dma_start(out=b2bc.rearrange("p (o n) -> p o n", o=1),
                        in_=ap["b2"].rearrange("(o n) -> o n", o=1)
                        .partition_broadcast(128))
        I(i, "GW1", 16)
    pm.em("gpsimd", fgrow2)
    uev["g_b2"] = ("GW1", bump("GW1", 16))
    t_h3, sc_g = dense_tm(
        "h3", "WB0", "xn", FW, "P3", "tanh_silu",
        tw0=[("V", v_he2[CH - 1]), ("V", v_wcG[11])],
        sw0=[("V", v_wcG[15])])


    v_prod, s_store = {}, {}

    def emit_E_tile(k):
        def fvpr(e, _k=k, _pre=cnt["V"]):
            W(e, "SC", sc_g[_k])
            if _k == 0:
                W(e, "V", v_wcG[15])
                W(e, "SC", sc_g[NTO - 1])
            if _k >= 2:
                W(e, *s_store[_k - 2])
            i = e.tensor_tensor(SGg[0][:, 1024 * (_k % 2):1024 * (_k % 2) + 1024],
                                T["P2"][:, 1024 * _k:1024 * (_k + 1)],
                                T["P3"][:, 1024 * _k:1024 * (_k + 1)], op=ALU.mult)
            AW(i, "V", _pre)
            I(i, "V", 1)
        pm.em("vector", fvpr)
        v_prod[k] = bump("V")

        def fss(e, _k=k):
            W(e, "V", v_prod[_k])
            if _k <= 1:
                W(e, "B1R", g_xcopy)
            I(e.dma_start(out=ap["out"][128 * _k:128 * (_k + 1), :],
                          in_=SGg[0][:, 1024 * (_k % 2):1024 * (_k % 2) + 1024],
                          accum_op=ALU.add), f"SO{_k % 2}", 16)
        pm.em("gpsimd", fss)
        s_store[k] = (f"SO{k % 2}", bump(f"SO{k % 2}", 16))

    t_hs, v_hes = dense_tm(
        "s", "WB1", "P4", FW, "P2", "copy",
        tw0=[("V", v_kadd[CH - 1]), ("V", v_wcG[15]),
             ("V", uev["v_add_h3"][NTO - 2])],
        tw1=[("V", uev["v_add_h3"][NTO - 1])],
        per_tile=emit_E_tile, badd="g_b2")

    so0_final, so1_final = cnt["SO0"], cnt["SO1"]

    def fend(e):
        W(e, "SO0", so0_final)
        W(e, "SO1", so1_final)
    pm.em("sync", fend)
    run_phase(pm)
    esM.close()

    es.close()
    return nc


_NC_CACHE = None


def kernel(L0_attn_input_data, L0_attn_W1, L0_attn_bias1, L0_attn_W2, L0_attn_bias2,
           L0_attn_memory_weights, L0_ffn_weight, L0_ffn_aux_weight1,
           L0_ffn_aux_weight2, L0_ffn_bias):
    global _NC_CACHE
    x = np.ascontiguousarray(np.asarray(L0_attn_input_data, dtype=np.float32)[0])
    tabs = _make_tables()
    cst1 = np.zeros((128, 1419), dtype=np.float32)
    cst1[:, 0:128] = np.eye(128, dtype=np.float32)
    cst1[0, 128:256] = 1.0
    off = 256
    for nm in ("tah", "tal", "te1h", "te1l", "te2h", "te2l", "tb1h", "tb1l",
               "tb2h", "tb2l"):
        t = tabs[nm]
        cst1[:, off:off + t.shape[1]] = t
        off += t.shape[1]
    for u in range(1, 41):
        cst1[u, 1380:1380 + min(u - 1, 39)] = 1.0
    mw = np.asarray(L0_attn_memory_weights, dtype=np.float32)
    b1 = np.asarray(L0_attn_bias1, dtype=np.float32)
    b2 = np.asarray(L0_attn_bias2, dtype=np.float32)
    fb = np.asarray(L0_ffn_bias, dtype=np.float32)
    cst2 = np.zeros((128, 2081), dtype=np.float32)
    cst2[0:41, 0:1024] = mw
    cst2[0:41, 1024:2048] = mw[0]
    cst2[:, 2048:2056] = b1.reshape(8, 128).T
    cst2[:, 2056:2064] = b2.reshape(8, 128).T
    cst2[:, 2064:2072] = fb.reshape(8, 128).T
    cst2[:, 2072:2080] = mw[0].reshape(8, 128).T
    cst2[:, 2080] = 1.0
    shared = {
        "W1": np.ascontiguousarray(L0_attn_W1, dtype=np.float32),
        "W2": np.ascontiguousarray(L0_attn_W2, dtype=np.float32),
        "Wf": np.ascontiguousarray(L0_ffn_weight, dtype=np.float32),
        "A1": np.ascontiguousarray(L0_ffn_aux_weight1, dtype=np.float32),
        "A2": np.ascontiguousarray(L0_ffn_aux_weight2, dtype=np.float32),
        "b1": b1, "b2": b2, "fb": fb,
        "cst1": cst1,
    }
    in_maps = []
    for c in range(8):
        lo = c * 2048 - 128
        xs = np.zeros((2176, D), dtype=np.float32)
        src_lo = max(lo, 0)
        xs[src_lo - lo:] = x[src_lo:c * 2048 + 2048]
        c2 = cst2.copy()
        c2[:, 2080] = 1.0 if c == 0 else 0.0
        im = dict(shared)
        im["x"] = xs
        im["cst2"] = np.ascontiguousarray(c2)
        in_maps.append(im)
    if _NC_CACHE is None:
        _NC_CACHE = build_nc()
    res = run_bass_kernel_spmd(_NC_CACHE, in_maps, core_ids=list(range(8)))
    out = np.concatenate([res.results[c]["out"] for c in range(8)], axis=0)
    return out[None].astype(np.float32)


# version 36 (final)
# speedup vs baseline: 1.0240x; 1.0240x over previous
import numpy as np
import concourse.bass as bass
import concourse.mybir as mybir
from concourse.bass_utils import run_bass_kernel_spmd
from contextlib import ExitStack

F32 = mybir.dt.float32
BF16 = mybir.dt.bfloat16
AF = mybir.ActivationFunctionType
ALU = mybir.AluOpType
AX = mybir.AxisListType

D = 1024
CH = 8
NT = 17
NTO = 16
NB = 24
TAPS = 40
VALID = 89
XNW = 2176
FW = 2048
EPS = 1e-5
ENGS = ("sync", "gpsimd", "tensor", "vector", "scalar")


def _hilo(t64):
    hi = t64.astype(np.float32)
    lo = (t64 - hi.astype(np.float64)).astype(np.float32)
    return hi, lo


def _make_tables():
    N = 128
    w = 2.0 * np.pi / N
    s = np.arange(N, dtype=np.float64)[:, None]
    acat = np.zeros((N, 128), dtype=np.float64)
    acat[:, 0:65] = np.cos(w * s * np.arange(0, 65)[None, :])
    acat[:, 65:128] = -np.sin(w * s * np.arange(1, 64)[None, :])
    t_i = np.arange(TAPS, dtype=np.float64)[:, None]
    e1 = np.zeros((128, 128), dtype=np.float64)
    e2 = np.zeros((128, 128), dtype=np.float64)
    e1[1:41, 0:65] = np.cos(w * t_i * np.arange(0, 65)[None, :])
    e1[1:41, 65:128] = -np.sin(w * t_i * np.arange(1, 64)[None, :])
    e2[1:41, 0:65] = -np.sin(w * t_i * np.arange(0, 65)[None, :])
    e2[1:41, 65:128] = np.cos(w * t_i * np.arange(1, 64)[None, :])
    u = np.arange(VALID, dtype=np.float64)[None, :] + (TAPS - 1)
    fcol = np.arange(0, 65, dtype=np.float64)[:, None]
    bre = np.cos(w * fcol * u) / N
    bre[1:64] *= 2.0
    bim = -2.0 * np.sin(w * fcol * u) / N
    bim[0] = 0.0
    bim[64] = 0.0
    bc1 = np.zeros((128, VALID), dtype=np.float64)
    bc2 = np.zeros((128, VALID), dtype=np.float64)
    bc1[0:65] = bre
    bc1[65:128] = -bre[1:64]
    bc2[0:65] = bim
    bc2[65:128] = bim[1:64]
    out = {}
    for nm, t in (("ta", acat), ("te1", e1), ("te2", e2), ("tb1", bc1), ("tb2", bc2)):
        hi, lo = _hilo(t)
        out[nm + "h"], out[nm + "l"] = hi, lo
    return out


class Prog:
    def __init__(self):
        self.steps = {e: [] for e in ENGS}

    def em(self, eng, fn):
        self.steps[eng].append(fn)


def build_nc():
    nc = bass.Bass("TRN2", target_bir_lowering=False)
    ap = {}
    ap["x"] = nc.dram_tensor("x", [2176, D], F32, kind="ExternalInput").ap()
    for nm in ("W1", "W2", "Wf", "A1", "A2"):
        ap[nm] = nc.dram_tensor(nm, [D, D], F32, kind="ExternalInput").ap()
    for nm in ("b1", "b2", "fb"):
        ap[nm] = nc.dram_tensor(nm, [D], F32, kind="ExternalInput").ap()
    ap["cst1"] = nc.dram_tensor("cst1", [128, 1419], F32, kind="ExternalInput").ap()
    ap["cst2"] = nc.dram_tensor("cst2", [128, 2081], F32, kind="ExternalInput").ap()
    ap["out"] = nc.dram_tensor("out", [2048, D], F32, kind="ExternalOutput").ap()
    ap["cci"] = nc.dram_tensor("cci", [128, 8], F32).ap()
    ap["cco"] = nc.dram_tensor("cco", [128, 8], F32).ap()

    es = ExitStack()
    T = {}

    def alloc(name, shape, dt, stack=None):
        T[name] = (stack or es).enter_context(nc.sbuf_tensor(name, shape, dt))
        return T[name]

    alloc("xn", [128, CH * XNW], BF16)
    for wb in ("WB0", "WB1", "WB2"):
        alloc(wb, [128, CH * 1024], BF16)
    alloc("identb", [128, 128], BF16)
    alloc("onesb", [128, 128], BF16)
    for nm in ("b1fm", "b2fm", "fbfm", "mw0fm", "ccsb", "totfm", "Kfm"):
        alloc(nm, [128, 8], F32)
    alloc("maskc", [128, 1], F32)
    alloc("onescol", [128, 1], BF16)
    alloc("ustat", [128, 16], F32)
    alloc("ub", [128, 8], BF16)
    alloc("ulb", [128, 8], BF16)
    alloc("sigs", [128, 2 * D], BF16)
    alloc("wfull", [128, D], BF16)
    alloc("tam", [128, 128], BF16)
    alloc("te1m", [128, 128], BF16)
    alloc("te2m", [128, 128], BF16)
    alloc("tb1m", [128, VALID], BF16)
    alloc("tb2m", [128, VALID], BF16)

    sems = {}
    semnames = (["GP", "CC", "T", "V", "SC", "CSTA", "CSTB", "CSTC", "GCC"]
                + [f"XT{j}" for j in range(5)] + [f"XG{j}" for j in range(5)]
                + ["SGA0", "SGA1", "SGA2", "SGA3", "GW0", "GW1", "GW2", "B1R",
                   "XE0", "XE1", "SO0", "SO1"])
    for s_ in semnames:
        sems[s_] = es.enter_context(nc.semaphore(s_))
    cnt = {s_: 0 for s_ in sems}

    def bump(sem, amt=1):
        cnt[sem] += amt
        return cnt[sem]

    def run_phase(prog):
        with nc.Block() as block:
            for eng in ENGS:
                steps = prog.steps[eng]
                if steps:
                    def body(e, _steps=steps):
                        for fn in _steps:
                            fn(e)
                    getattr(block, eng)(body)

    def W(e, sem, val=None):
        if val is None:
            sem, val = sem
        e.wait_ge(sems[sem], val)

    def I(instr, sem, amt=1):
        instr.then_inc(sems[sem], amt)

    def AW(instr, sem, val=None):
        if val is None:
            sem, val = sem
        instr._wait_ge(sems[sem], val)
        return instr

    alloc("P4", [128, CH * FW], BF16)
    pblk = [alloc("pblk0", [128, D], BF16)] * 2
    Xe = [alloc("Xe0", [128, D], BF16)] * 2
    pk1 = [alloc("pk1_0", [128, D], BF16)] * 2
    pk2 = [alloc("pk2_0", [128, D], BF16)] * 2
    SGg = [alloc("sgg0", [128, 2048], F32)] * 2
    PS6 = es.enter_context(nc.psum_tensor("PS6", [128, 3072], F32))

    def psP(a, b):
        return PS6[:, a:b]

    def psXs(a, b):
        return PS6[:, 1024 + a:1024 + b]

    def psFa(a, b):
        return PS6[:, 2048 + a:2048 + b]

    def psFb(a, b):
        return PS6[:, 2560 + a:2560 + b]

    pa = Prog()
    esA = ExitStack()
    xt = [esA.enter_context(nc.sbuf_tensor(f"xt{i}", [128, D], F32)) for i in range(4)]
    xnt = [esA.enter_context(nc.sbuf_tensor(f"xnt{i}", [128, D], BF16)) for i in range(3)]
    sqs = esA.enter_context(nc.sbuf_tensor("sqs", [128, D], F32))
    st_ = esA.enter_context(nc.sbuf_tensor("stats", [128, 64], F32))
    SG = [esA.enter_context(nc.sbuf_tensor(f"sg{i}", [128, 2048], F32)) for i in range(4)]
    cst1b = esA.enter_context(nc.sbuf_tensor("cst1b", [128, 1419], F32))
    RSb = esA.enter_context(nc.sbuf_tensor("RSb", [128, 312], F32))
    Ubm = esA.enter_context(nc.sbuf_tensor("Ubm", [128, 39], BF16))
    KBv = esA.enter_context(nc.sbuf_tensor("KBv", [128, 8], F32))
    bc0 = esA.enter_context(nc.sbuf_tensor("bc0", [128, 8], F32))
    cst2v = esA.enter_context(nc.sbuf_tensor("cst2v", [128, 33], F32))
    psT = [esA.enter_context(nc.psum_tensor("psT0", [128, D], BF16))] * 2
    psU = esA.enter_context(nc.psum_tensor("psU", [128, 512], F32))
    psT2 = PS6[:, 2048:2560].bitcast(BF16)
    psU2 = PS6[:, 2560:2568]

    def psT_of(k):
        return psT2 if (k < 8 and k % 2 == 1) else psT[0][:, :]

    def psU_of(k):
        return psU2 if (k < 8 and k % 2 == 0) else psU[:, 0:8]
    uev = {}

    s_w, s_x, v_wc, v_norm, v_evac, t_tr = {}, {}, {}, {}, {}, {}
    v_uacc = {}
    sc_sq_map = {}
    stats_v = {}

    sync_order = [("x", 4), ("x", 6), ("w", "W1", 0), ("x", 8),
                  ("w", "W1", 1), ("x", 10)]
    sync_tail = [[("x", 12)], [("x", 14)], [("x", 16)], []]
    gpsimd_order = [("x", 1), ("x", 2), ("x", 3), ("x", 5), ("w", "W1", 2),
                    ("x", 7), ("w", "W1", 3), ("x", 9), ("x", 11), ("x", 13),
                    ("x", 15)]
    W1SLOT = {0: 0, 1: 1, 2: 2, 3: 3}

    def emit_loads(eng, order):
        for item in order:
            if item[0] == "w":
                _, wnm, i = item

                def fs(e, _g=i, _w=wnm, _i=i):
                    instr = e.dma_start(
                        out=SG[_g][:, :],
                        in_=ap[_w][256 * _i:256 * (_i + 1), :].rearrange(
                            "(a p) n -> p a n", p=128))
                    I(instr, f"SGA{_g}", 16)
                pa.em(eng, fs)
                s_w[i] = (f"SGA{i}", bump(f"SGA{i}", 16))
            else:
                _, k = item
                sx = ("XT" if eng == "sync" else "XG") + str(k % 4)

                def fx(e, _k=k, _sx=sx):
                    if _k >= 4:
                        W(e, "SC", v_norm[_k - 4])
                    I(e.dma_start(out=xt[_k % 4][:, :],
                                  in_=ap["x"][128 * _k:128 * (_k + 1), :]),
                      _sx, 16)
                pa.em(eng, fx)
                s_x[k] = (sx, bump(sx, 16))

    def f_gmemset(e):
        I(e.memset(T["wfull"][:, :], 0.0), "GP", 1)
        I(e.memset(T["onescol"][:, :], 1.0), "GP", 1)
        I(e.memset(T["ustat"][:, 0:8], 0.0), "GP", 1)
    pa.em("gpsimd", f_gmemset)
    g_memset = bump("GP", 3)

    emit_loads("sync", [("x", 0)])

    def f_cst1(e):
        I(e.dma_start(out=cst1b[:, :], in_=ap["cst1"]), "CSTA", 16)
    pa.em("sync", f_cst1)
    csta1 = bump("CSTA", 16)

    def f_cst2(e):
        I(e.dma_start(out=SGg[0][:, :], in_=ap["cst2"][:, 0:2048]), "CSTC", 16)
        I(e.dma_start(out=cst2v[:, :], in_=ap["cst2"][:, 2048:2081]), "CSTB", 16)
    pa.em("sync", f_cst2)
    cstc_mw = ("CSTC", bump("CSTC", 16))
    csta_final = ("CSTB", bump("CSTB", 16))

    emit_loads("sync", sync_order)
    emit_loads("gpsimd", gpsimd_order)

    C1OFF = {"ident": 0, "ones": 128, "tah": 256, "tal": 384, "te1h": 512,
             "te1l": 640, "te2h": 768, "te2l": 896, "tb1h": 1024, "tb1l": 1113,
             "tb2h": 1202, "tb2l": 1291, "ub": 1380}

    def c1(nm, w=128):
        o = C1OFF[nm]
        return cst1b[:, o:o + w]

    def fvid_emit():
      def fvid(e, _pre=cnt["V"]):
        W(e, "CSTA", csta1)
        AW(e.tensor_copy(T["identb"][:, :], c1("ident")), "V", _pre)
        for mrg, hi, lo in (("tam", "tah", "tal"), ("te1m", "te1h", "te1l"),
                            ("te2m", "te2h", "te2l")):
            e.tensor_tensor(T[mrg][:, :], c1(hi), c1(lo), op=ALU.add)
        for mrg, hi, lo in (("tb1m", "tb1h", "tb1l"), ("tb2m", "tb2h", "tb2l")):
            e.tensor_tensor(T[mrg][:, :], c1(hi, VALID), c1(lo, VALID), op=ALU.add)
        e.tensor_copy(Ubm[:, :], c1("ub", 39))
        I(e.tensor_copy(T["onesb"][:, :], c1("ones")), "V", 1)
      pa.em("vector", fvid)
      fm["v_identb"] = bump("V")

    def fvm_emit():
      def fvm(e, _pre=cnt["V"]):
        W(e, *csta_final)
        W(e, *cstc_mw)
        i = e.tensor_copy(T["b1fm"][:, :], cst2v[:, 0:8])
        AW(i, "V", _pre)
        e.tensor_copy(T["b2fm"][:, :], cst2v[:, 8:16])
        e.tensor_copy(T["fbfm"][:, :], cst2v[:, 16:24])
        e.tensor_copy(T["mw0fm"][:, :], cst2v[:, 24:32])
        i = e.tensor_copy(T["maskc"][:, :], cst2v[:, 32:33])
        I(i, "V", 1)
        i = e.tensor_tensor(T["wfull"][0:41, :], SGg[0][0:41, 0:1024],
                            SGg[0][0:41, 1024:2048], op=ALU.subtract)
        AW(i, "GP", g_memset)
        I(i, "V", 1)
      pa.em("vector", fvm)
      fm["v_fmv"] = bump("V")
      fm["v_wfull"] = bump("V")

    fm = {}

    def emit_rs(k):
        def fs1(e, _k=k, _pre=cnt["SC"]):
            W(e, *s_x[_k])
            i = e.activation(sqs[:, :], xt[_k % 4][:, :], AF.Identity,
                             accum_out=st_[:, _k % 4:_k % 4 + 1])
            AW(i, "SC", _pre)
            I(i, "SC", 1)
            i = e.activation(sqs[:, :], xt[_k % 4][:, :], AF.Square,
                             accum_out=st_[:, 4 + _k % 4:5 + _k % 4])
            AW(i, "SC", _pre + 1)
            I(i, "SC", 1)
        pa.em("scalar", fs1)
        bump("SC")
        sc_sq_map[k] = bump("SC")

    def emit_stats(tiles):
        nj = len(tiles)

        def fv2g(e, _sc=sc_sq_map[tiles[-1]], _v0=cnt["V"], _nj=nj):
            W(e, "SC", _sc)
            v = _v0
            i = e.tensor_scalar(st_[:, 8:8 + _nj], st_[:, 0:_nj], 1.0 / D, None,
                                op0=ALU.mult)
            AW(i, "V", v); I(i, "V", 1); v += 1
            i = e.tensor_tensor(st_[:, 12:12 + _nj], st_[:, 8:8 + _nj],
                                st_[:, 8:8 + _nj], op=ALU.mult)
            AW(i, "V", v); I(i, "V", 1); v += 1
            i = e.scalar_tensor_tensor(st_[:, 16:16 + _nj], st_[:, 4:4 + _nj],
                                       1.0 / D, st_[:, 12:12 + _nj],
                                       op0=ALU.mult, op1=ALU.subtract)
            AW(i, "V", v); I(i, "V", 1); v += 1
            i = e.tensor_scalar(st_[:, 16:16 + _nj], st_[:, 16:16 + _nj], EPS, None,
                                op0=ALU.add)
            AW(i, "V", v); I(i, "V", 1); v += 1
            i = e.reciprocal(st_[:, 20:20 + _nj], st_[:, 16:16 + _nj])
            AW(i, "V", v); I(i, "V", 1); v += 1
        pa.em("vector", fv2g)
        v_rec = bump("V", 5)

        def fs2g(e, _v=v_rec, _pre=cnt["SC"], _nj=nj):
            W(e, "V", _v)
            i = e.activation(st_[:, 24:24 + _nj], st_[:, 20:20 + _nj], AF.Sqrt)
            AW(i, "SC", _pre)
            I(i, "SC", 1)
        pa.em("scalar", fs2g)
        sc_sqrt = bump("SC")

        def fv3g(e, _sc=sc_sqrt, _v0=cnt["V"], _nj=nj):
            W(e, "SC", _sc)
            v = _v0
            i = e.tensor_tensor(st_[:, 28:28 + _nj], st_[:, 24:24 + _nj],
                                st_[:, 24:24 + _nj], op=ALU.mult)
            AW(i, "V", v); I(i, "V", 1); v += 1
            i = e.tensor_tensor(st_[:, 28:28 + _nj], st_[:, 28:28 + _nj],
                                st_[:, 16:16 + _nj], op=ALU.mult)
            AW(i, "V", v); I(i, "V", 1); v += 1
            i = e.tensor_scalar(st_[:, 28:28 + _nj], st_[:, 28:28 + _nj], -0.5,
                                1.5, op0=ALU.mult, op1=ALU.add)
            AW(i, "V", v); I(i, "V", 1); v += 1
            i = e.tensor_tensor(st_[:, 32:32 + _nj], st_[:, 24:24 + _nj],
                                st_[:, 28:28 + _nj], op=ALU.mult)
            AW(i, "V", v); I(i, "V", 1); v += 1
            i = e.scalar_tensor_tensor(st_[:, 36:36 + _nj], st_[:, 8:8 + _nj],
                                       -1.0, st_[:, 32:32 + _nj],
                                       op0=ALU.mult, op1=ALU.mult)
            AW(i, "V", v); I(i, "V", 1); v += 1
        pa.em("vector", fv3g)
        stats_v[tiles[0] // 4] = bump("V", 5)

    def emit_nm(k):
        def fnorm(e, _k=k, _pre=cnt["SC"]):
            W(e, "V", stats_v[_k // 4])
            if _k >= 3:
                W(e, "T", t_tr[_k - 3])
            j = _k % 4
            i = e.activation(xnt[_k % 3][:, :], xt[_k % 4][:, :], AF.Identity,
                             scale=st_[:, 32 + j:33 + j],
                             bias=st_[:, 36 + j:37 + j])
            AW(i, "SC", _pre)
            I(i, "SC", 1)
        pa.em("scalar", fnorm)
        v_norm[k] = bump("SC")

    def emit_tf(k):
        def ft(e, _k=k, _pre=cnt["T"]):
            W(e, "SC", v_norm[_k])
            if _k == 0:
                W(e, "V", fm["v_identb"])
            if _k == 1:
                W(e, "GP", g_memset)
            pT = _k - 2 if _k <= 8 else _k - 1
            if pT >= 0:
                W(e, "V", v_evac[pT])
            pU = _k - 2 if _k < 8 else _k - 1
            if pU >= 1:
                W(e, "V", v_uacc[pU])
            dstT = psT_of(_k)
            first = True
            for c in range(CH):
                instr = e.transpose(dstT[:, 128 * c:128 * (c + 1)],
                                    xnt[_k % 3][:, 128 * c:128 * (c + 1)],
                                    T["identb"][:, :])
                if first:
                    AW(instr, "T", _pre)
                    first = False
            if _k >= 1:
                dstU = psU_of(_k)
                for c in range(CH):
                    instr = e.matmul(dstU[:, c:c + 1],
                                     xnt[_k % 3][:, 128 * c:128 * (c + 1)],
                                     T["onescol"][:, :],
                                     start=True, stop=True)
            I(instr, "T", 1)
        pa.em("tensor", ft)
        t_tr[k] = bump("T")

        def fve(e, _k=k, _pre=cnt["V"]):
            W(e, "T", t_tr[_k])
            xn3 = T["xn"][:, 0:CH * XNW].rearrange("p (c w) -> p c w", c=CH)
            ps3 = psT_of(_k).rearrange("p (c w) -> p c w", c=CH)
            instr = e.tensor_copy(xn3[:, :, 128 * _k:128 * (_k + 1)], ps3)
            AW(instr, "V", _pre)
            I(instr, "V", 1)
        pa.em("vector", fve)
        v_evac[k] = bump("V")

        if k >= 1:
            def fuacc(e, _k=k, _pre=cnt["V"]):
                if _k == 1:
                    W(e, "GP", g_memset)
                instr = e.tensor_tensor(T["ustat"][:, 0:8], T["ustat"][:, 0:8],
                                        psU_of(_k), op=ALU.add)
                AW(instr, "V", _pre)
                I(instr, "V", 1)
            pa.em("vector", fuacc)
            v_uacc[k] = bump("V")

    def emit_wc_w1(pc):
        def fvw(e, _pc=pc, _pre=cnt["V"]):
            W(e, *s_w[_pc])
            i = e.tensor_copy(T["WB0"][:, 2048 * _pc:2048 * (_pc + 1)],
                              SG[_pc][:, :])
            AW(i, "V", _pre)
            I(i, "V", 1)
        pa.em("vector", fvw)
        v_wc[pc] = bump("V")

    emit_rs(0)
    fvid_emit()
    for k in (1, 2, 3):
        emit_rs(k)
    emit_stats([0, 1, 2, 3])
    for k in (0, 1, 2, 3):
        emit_nm(k)
        emit_tf(k)
    fvm_emit()
    for k in (4, 5, 6, 7):
        emit_rs(k)
    emit_stats([4, 5, 6, 7])
    for k in (4, 5, 6, 7):
        emit_nm(k)
        emit_tf(k)
    for pc in range(4):
        emit_wc_w1(pc)

    def ftc0(e, _pre=cnt["T"]):
        W(e, "V", fm["v_wfull"])
        W(e, "GP", g_memset)
        first = True
        for half in range(2):
            i = e.matmul(psXs(512 * half, 512 * (half + 1)), T["te1m"][:, :],
                         T["wfull"][:, 512 * half:512 * (half + 1)], start=True,
                         stop=True)
            if first:
                AW(i, "T", _pre)
                first = False
        for half in range(2):
            instr = e.matmul(psP(512 * half, 512 * (half + 1)), T["te2m"][:, :],
                             T["wfull"][:, 512 * half:512 * (half + 1)], start=True,
                             stop=True)
        I(instr, "T", 1)
    pa.em("tensor", ftc0)
    t_sig = bump("T")

    def fvc1(e, _pre=cnt["V"]):
        W(e, "T", t_sig)
        i = e.tensor_copy(T["sigs"][:, 0:D], psXs(0, 1024))
        AW(i, "V", _pre)
        I(e.tensor_copy(T["sigs"][:, D:2 * D], psP(0, 1024)), "V", 1)
    pa.em("vector", fvc1)
    v_sig = bump("V")

    def ftS(e, _pre=cnt["T"]):
        W(e, "V", v_sig)
        first = True
        for c in range(CH):
            i = e.matmul(psXs(39 * c, 39 * c + 39),
                         T["wfull"][:, 128 * c:128 * (c + 1)], Ubm[:, :],
                         start=True, stop=True)
            if first:
                AW(i, "T", _pre)
                first = False
        for c in range(CH):
            instr = e.matmul(psXs(960 + c, 961 + c),
                             T["wfull"][:, 128 * c:128 * (c + 1)],
                             T["onescol"][:, :], start=True, stop=True)
        I(instr, "T", 1)
    pa.em("tensor", ftS)
    t_S = bump("T")

    def fvS(e, _v0=cnt["V"]):
        W(e, "T", t_S)
        v = _v0
        i = e.tensor_copy(RSb[:, :], psXs(0, 312))
        AW(i, "V", v); I(i, "V", 1); v += 1
        i = e.tensor_tensor(KBv[:, :], T["b1fm"][:, :], psXs(960, 968),
                            op=ALU.mult)
        AW(i, "V", v); I(i, "V", 1); v += 1
        i = e.tensor_scalar(bc0[:, :], T["b1fm"][:, :], T["maskc"][:, 0:1],
                            -1.0, op0=ALU.mult, op1=ALU.mult)
        AW(i, "V", v); I(i, "V", 1); v += 1
    pa.em("vector", fvS)
    v_S = bump("V", 3)

    g_wp, v_wcG = {}, {}
    WPIECES = [("Wf", i) for i in range(4)] + [("A1", i) for i in range(4)] \
        + [("A2", i) for i in range(4)] + [("W2", i) for i in range(4)]
    WDEST = {"Wf": "WB1", "A1": "WB2", "A2": "WB0", "W2": "WB1"}

    def wp_buf(gi):
        return SG[0] if (gi < 8 and gi % 2 == 1) else SGg[0]

    def emit_wpiece(gi, prog):
        wnm, i = WPIECES[gi]

        def fgw(e, _g=gi, _w=wnm, _i=i):
            if _g == 0:
                W(e, "V", fm["v_wfull"])
            elif _g == 1:
                W(e, "V", v_wc[0])
            elif _g < 8:
                W(e, "V", v_wcG[_g - 2])
            else:
                W(e, "V", v_wcG[_g - 1])
            _sem = "GW2" if (_g < 8 and _g % 2 == 1) else "GW0"
            I(e.dma_start(out=wp_buf(_g)[:, :],
                          in_=ap[_w][256 * _i:256 * (_i + 1), :].rearrange(
                              "(a p) n -> p a n", p=128)), _sem, 16)
        prog.em("sync", fgw)
        sem = "GW2" if (gi < 8 and gi % 2 == 1) else "GW0"
        g_wp[gi] = (sem, bump(sem, 16))

    def emit_wcast(gi, prog):
        wdst = WDEST[WPIECES[gi][0]]
        po = WPIECES[gi][1]

        def fvwc(e, _g=gi, _wd=wdst, _po=po, _pre=cnt["V"]):
            W(e, *g_wp[_g])
            inst = e.tensor_copy(T[_wd][:, 2048 * _po:2048 * (_po + 1)],
                                 wp_buf(_g)[:, :])
            AW(inst, "V", _pre)
            I(inst, "V", 1)
        prog.em("vector", fvwc)
        v_wcG[gi] = bump("V")

    for gi in range(8):
        emit_wpiece(gi, pa)
        emit_loads("sync", sync_tail[gi]) if gi < 4 else None

    def emit_u_tail():
        def fvut(e, _v0=cnt["V"]):
            v = _v0
            i = e.tensor_copy(T["ub"][:, :], T["ustat"][:, 0:8])
            AW(i, "V", v); I(i, "V", 1); v += 1
            i = e.tensor_tensor(T["ustat"][:, 8:16], T["ustat"][:, 0:8],
                                T["ub"][:, :], op=ALU.subtract)
            AW(i, "V", v); I(i, "V", 1); v += 1
            i = e.tensor_copy(T["ulb"][:, :], T["ustat"][:, 8:16])
            AW(i, "V", v); I(i, "V", 1); v += 1
        pa.em("vector", fvut)
        uev["v_u"] = bump("V", 3)

        def ftu(e, _pre=cnt["T"]):
            W(e, "V", uev["v_u"])
            first = True
            for ec in range(CH):
                for dk in range(CH):
                    stat = T["WB0"][:, dk * 1024 + ec * 128: dk * 1024 + (ec + 1) * 128]
                    i = e.matmul(psU[:, ec:ec + 1], stat,
                                 T["ub"][:, dk:dk + 1], start=(dk == 0), stop=False)
                    if first:
                        AW(i, "T", _pre)
                        first = False
                    instr = e.matmul(psU[:, ec:ec + 1], stat,
                                     T["ulb"][:, dk:dk + 1], start=False,
                                     stop=(dk == CH - 1))
            I(instr, "T", 1)
        pa.em("tensor", ftu)
        uev["t_u"] = bump("T")

        def fvp(e, _pre=cnt["V"]):
            W(e, "T", uev["t_u"])
            i = e.scalar_tensor_tensor(T["ccsb"][:, :], T["b1fm"][:, :], 2048.0,
                                       psU[:, 0:8], op0=ALU.mult, op1=ALU.add)
            AW(i, "V", _pre)
            I(i, "V", 1)
        pa.em("vector", fvp)
        uev["v_partial"] = bump("V")

    t_p, t_X, t_f, sc_p, sc_X, v_pk, v_fe = {}, {}, {}, {}, {}, {}, {}

    def tmax_of(i):
        return min(NT - 1, (VALID * (i + 1) + 127) // 128) if i < NB - 1 else NT - 1

    def emit_conv_te(i):
        if i <= NB - 1:
            assert tmax_of(i) in v_evac, f"conv block {i} needs un-emitted tile"

            def ftp(e, _i=i, _pre=cnt["T"]):
                if _i == 0:
                    W(e, "V", v_sig)
                    W(e, "V", v_wc[3])
                else:
                    W(e, "SC", sc_p[_i - 1])
                W(e, "V", v_evac[tmax_of(_i)])
                so = VALID * (_i + 1) if _i < NB - 1 else 2048
                first = True
                for dk in range(CH):
                    for half in range(2):
                        instr = e.matmul(
                            psP(512 * half, 512 * (half + 1)),
                            T["xn"][:, dk * XNW + so: dk * XNW + so + 128],
                            T["WB0"][:, dk * 1024 + 512 * half: dk * 1024 + 512 * (half + 1)],
                            start=(dk == 0), stop=(dk == CH - 1))
                        if first:
                            AW(instr, "T", _pre)
                            first = False
                I(instr, "T", 1)
            pa.em("tensor", ftp)
            t_p[i] = bump("T")
        if 1 <= i <= NB:
            b = i - 1

            def ftx(e, _b=b, _pre=cnt["T"]):
                if _b == 0:
                    W(e, "SC", sc_p[0])
                    W(e, "V", v_S)
                else:
                    W(e, "SC", sc_p[_b])
                    W(e, "V", v_pk[_b - 1])
                first = True
                for half in range(2):
                    instr = e.matmul(psXs(512 * half, 512 * (half + 1)),
                                     T["tam"][:, :],
                                     pblk[_b % 2][:, 512 * half:512 * (half + 1)],
                                     start=True, stop=True)
                    if first:
                        AW(instr, "T", _pre)
                        first = False
                I(instr, "T", 1)
            pa.em("tensor", ftx)
            t_X[b] = bump("T")
        if 2 <= i <= NB + 1:
            b2 = i - 2

            def ftf(e, _b=b2, _pre=cnt["T"]):
                W(e, "V", v_pk[_b])
                if _b >= 1:
                    W(e, "V", v_fe[_b - 1])
                first = True
                for dk in range(CH):
                    pf = (psFa(89 * dk, 89 * dk + 89) if dk < 5
                          else psFb(89 * (dk - 5), 89 * (dk - 5) + 89))
                    inst = e.matmul(pf, pk1[_b % 2][:, dk * 128:(dk + 1) * 128],
                                    T["tb1m"][:, :], start=True, stop=False)
                    if first:
                        AW(inst, "T", _pre)
                        first = False
                    instr = e.matmul(pf, pk2[_b % 2][:, dk * 128:(dk + 1) * 128],
                                     T["tb2m"][:, :], start=False, stop=True)
                I(instr, "T", 1)
            pa.em("tensor", ftf)
            t_f[b2] = bump("T")

    def emit_conv_sc(i):
        if i <= NB - 1:
            def fsp(e, _i=i, _pre=cnt["SC"]):
                W(e, "T", t_p[_i])
                if _i >= 1:
                    W(e, "T", t_X[_i - 1])
                inst = e.copy(pblk[_i % 2][:, :], psP(0, 1024))
                AW(inst, "SC", _pre)
                I(inst, "SC", 1)
            pa.em("scalar", fsp)
            sc_p[i] = bump("SC")


    def emit_conv_v(i):
        if 2 <= i <= NB + 1:
            b2 = i - 2

            def fvfe(e, _b=b2, _pre=cnt["V"]):
                W(e, "T", t_f[_b])
                ln = VALID
                wo = VALID * _b if _b < NB - 1 else FW - VALID
                first = True
                for dk in range(CH):
                    pf = (psFa(89 * dk, 89 * dk + ln) if dk < 5
                          else psFb(89 * (dk - 5), 89 * (dk - 5) + ln))
                    instr = e.tensor_copy(
                        T["P4"][:, dk * FW + wo: dk * FW + wo + ln], pf)
                    if first:
                        AW(instr, "V", _pre)
                        first = False
                I(instr, "V", 1)
            pa.em("vector", fvfe)
            v_fe[b2] = bump("V")
        if 1 <= i <= NB:
            b = i - 1

            def fvpk(e, _b=b, _pre=cnt["V"]):
                W(e, "T", t_X[_b])
                if _b >= 1:
                    W(e, "T", t_f[_b - 1])
                inst = e.tensor_tensor(pk1[_b % 2][:, :], psXs(0, 1024),
                                       T["sigs"][:, 0:D], op=ALU.mult)
                AW(inst, "V", _pre)
                I(e.tensor_tensor(pk2[_b % 2][:, :], psXs(0, 1024),
                                  T["sigs"][:, D:2 * D], op=ALU.mult), "V", 1)
            pa.em("vector", fvpk)
            v_pk[b] = bump("V")

    units = []
    for g in (2, 3):
        a = 4 * g
        tiles = list(range(a, a + 4))
        for k in tiles:
            units.append(("rs", k))
        units.append(("st", tiles))
        units += [("nm", a), ("nm", a + 1), ("tf", a), ("nm", a + 2),
                  ("tf", a + 1), ("nm", a + 3), ("tf", a + 2), ("tf", a + 3)]
    units += [("rs", 16), ("st", [16]), ("nm", 16), ("tf", 16)]
    unit_sched = {it: [2 * it, 2 * it + 1] for it in range(10)}
    for it in range(10, 20):
        unit_sched[it] = [it + 10]
    wcC_sched = {6: 0, 8: 1, 10: 2, 12: 3, 14: 4, 16: 5, 18: 6, 20: 7}

    for i in range(NB + 3):
        emit_conv_te(i)
        emit_conv_sc(i)
        emit_conv_v(i)
        for uj in unit_sched.get(i, []):
            kind, arg = units[uj]
            if kind == "rs":
                emit_rs(arg)
            elif kind == "st":
                emit_stats(arg)
            elif kind == "nm":
                emit_nm(arg)
            else:
                emit_tf(arg)
        if i == 16:
            emit_u_tail()
        if i == 3:
            def fcorr(e, _pre=cnt["V"]):
                W(e, "V", v_fe[0])
                first = True
                for c in range(CH):
                    ins = e.scalar_tensor_tensor(
                        T["P4"][:, c * FW:c * FW + 39], RSb[:, 39 * c:39 * c + 39],
                        bc0[:, c:c + 1], T["P4"][:, c * FW:c * FW + 39],
                        op0=ALU.mult, op1=ALU.add)
                    if first:
                        AW(ins, "V", _pre)
                        first = False
                I(ins, "V", 1)
            pa.em("vector", fcorr)
            v_corr = bump("V")
        if i in wcC_sched:
            emit_wcast(wcC_sched[i], pa)

    def fgcc(e):
        i = e.dma_start(out=ap["cci"], in_=T["ccsb"][:, :])
        AW(i, "V", uev["v_partial"])
        I(i, "GCC", 16)
        i = e.collective_compute("AllReduce", ALU.add,
                                 replica_groups=[list(range(8))],
                                 ins=[ap["cci"].opt()], outs=[ap["cco"].opt()])
        AW(i, "GCC", g_ccin_v)
        I(i, "CC", 1)
        i = e.dma_start(out=T["totfm"][:, :], in_=ap["cco"])
        AW(i, "CC", cc_done_v)
        I(i, "GCC", 16)
    g_ccin_v = cnt["GCC"] + 16
    cc_done_v = cnt["CC"] + 1
    pa.em("gpsimd", fgcc)
    bump("GCC", 16)
    bump("CC", 1)
    g_tot = ("GCC", bump("GCC", 16))

    run_phase(pa)
    esA.close()
    alloc("P2", [128, CH * FW], BF16)
    alloc("P3", [128, CH * FW], BF16)
    b2bc = T["sigs"][:, :].bitcast(F32)
    pm = Prog()
    esM = ExitStack()
    psH2 = esM.enter_context(nc.psum_tensor("psH2", [128, 1024], F32))

    def psH(i, a, b):
        if i == 0:
            return PS6[:, a:b]
        if b <= 1024:
            return PS6[:, 2048 + a:2048 + b]
        assert a >= 1024
        return psH2[:, a - 1024:b - 1024]

    def psH4(i, a, b):
        assert b <= 1024
        if i < 3:
            return PS6[:, 1024 * i + a:1024 * i + b]
        return psH2[:, a:b]

    xt2 = [SGg[0][:, 0:1024], SGg[0][:, 1024:2048]]
    tt512 = SGg[0][:, 1024:1536]

    def dense_tm(name, wbuf, src, srcw, dst, evac_kind, tw0=(), tw1=(), sw0=(),
                 bias_ready=None, per_tile=None, badd=None):
        t_h, ev, v_add = {}, {}, {}
        bias_mm = False
        scr = [SGg[0][:, 0:1024], SGg[0][:, 1024:2048]]
        for tk in range(NTO):
            def fth(e, _t=tk, _pre=cnt["T"]):
                if _t == 0:
                    for wv in tw0:
                        W(e, *wv)
                    if bias_mm and bias_ready is not None:
                        W(e, "V", uev[bias_ready])
                if _t == 1:
                    for wv in tw1:
                        W(e, *wv)
                if _t < 4 and evac_kind != "tanh_silu":
                    W(e, "V", uev["v_add_h3"][NTO - 4 + _t])
                if _t >= 4:
                    if evac_kind == "tanh_silu":
                        W(e, "V", v_add[_t - 4])
                    else:
                        W(e, "V", ev[_t - 4])
                first = True
                for dk in range(CH):
                    for half in range(2):
                        instr = e.matmul(
                            psH4(_t % 4, 512 * half, 512 * (half + 1)),
                            T[src][:, dk * srcw + 128 * _t: dk * srcw + 128 * (_t + 1)],
                            T[wbuf][:, dk * 1024 + 512 * half: dk * 1024 + 512 * (half + 1)],
                            start=(dk == 0), stop=(False if bias_mm else dk == CH - 1))
                        if first:
                            AW(instr, "T", _pre)
                            first = False
                I(instr, "T", 1)
            pm.em("tensor", fth)
            t_h[tk] = bump("T")
            if evac_kind == "tanh_silu":
                def fva(e, _t=tk, _pre=cnt["V"]):
                    W(e, "T", t_h[_t])
                    if _t == 0:
                        for wv in sw0:
                            W(e, *wv)
                        W(e, *uev["g_fb"])
                    if _t >= 2:
                        W(e, "SC", ev[_t - 2])
                    i = e.tensor_tensor(scr[_t % 2][:, 0:1024],
                                        psH4(_t % 4, 0, 1024), b2bc, op=ALU.add)
                    AW(i, "V", _pre)
                    I(i, "V", 1)
                pm.em("vector", fva)
                v_add[tk] = bump("V")

                def fsh(e, _t=tk, _s0=cnt["SC"]):
                    W(e, "V", v_add[_t])
                    st = _s0
                    for sp in range(2):
                        sc_ = scr[_t % 2][:, 512 * sp:512 * (sp + 1)]
                        i = e.activation(sc_, sc_, AF.Tanh)
                        AW(i, "SC", st); I(i, "SC", 1); st += 1
                        i = e.activation(
                            T[dst][:, 1024 * _t + 512 * sp: 1024 * _t + 512 * (sp + 1)],
                            sc_, AF.Silu)
                        AW(i, "SC", st); I(i, "SC", 1); st += 1
                pm.em("scalar", fsh)
                ev[tk] = bump("SC", 4)
            else:
                def fvh(e, _t=tk, _pre=cnt["V"]):
                    W(e, "T", t_h[_t])
                    if badd is not None and _t == 0:
                        W(e, *uev[badd])
                    if badd is None:
                        i = e.tensor_copy(T[dst][:, 1024 * _t:1024 * (_t + 1)],
                                          psH4(_t % 4, 0, 1024))
                        AW(i, "V", _pre)
                    else:
                        i = e.tensor_tensor(T[dst][:, 1024 * _t:1024 * (_t + 1)],
                                            psH4(_t % 4, 0, 1024), b2bc,
                                            op=ALU.add)
                        AW(i, "V", _pre)
                    I(i, "V", 1)
                pm.em("vector", fvh)
                ev[tk] = bump("V")
            if per_tile is not None:
                per_tile(tk)
        uev["v_add_" + name] = dict(v_add)
        return t_h, ev

    def emit_vrow_cast(stage_cols, free_after, dma_evt):
        def fvr(e, _pre=cnt["V"], _c=stage_cols, _fa=free_after, _de=dma_evt):
            W(e, *uev[_de])
            W(e, "T", _fa)
            i = e.tensor_copy(T["b1row"][0:1, :], SGg[0][0:1, _c:_c + 1024])
            AW(i, "V", _pre)
            I(i, "V", 1)
        pm.em("vector", fvr)
        return bump("V")

    def dense_pass(name, wbuf, src, srcw, srco, dst, evac_kind,
                   tw0=(), tw1=(), vw0=(), sw0=(), evw=(), wload=None):
        t_h, v_he, sc_free = {}, {}, {}
        wcB_sched = {1: 0, 3: 1, 5: 2, 7: 3}
        if wload is not None:
            for gi in wload:
                emit_wpiece(gi, pm)
        for ec in range(CH):
            def fth(e, _e=ec, _pre=cnt["T"]):
                if _e == 0:
                    for wv in tw0:
                        W(e, *wv)
                if _e == 1:
                    for wv in tw1:
                        W(e, *wv)
                if _e >= 2:
                    if evac_kind == "tanh_silu":
                        W(e, "SC", sc_free[_e - 2])
                    else:
                        W(e, "V", v_he[_e - 2])
                first = True
                for dk in range(CH):
                    for sp in range(4):
                        instr = e.matmul(
                            psH(_e % 2, 512 * sp, 512 * (sp + 1)),
                            T[wbuf][:, dk * 1024 + _e * 128: dk * 1024 + (_e + 1) * 128],
                            T[src][:, dk * srcw + srco + 512 * sp:
                                   dk * srcw + srco + 512 * (sp + 1)],
                            start=(dk == 0), stop=(dk == CH - 1))
                        if first:
                            AW(instr, "T", _pre)
                            first = False
                I(instr, "T", 1)
            pm.em("tensor", fth)
            t_h[ec] = bump("T")
            if evac_kind in ("copy", "bias_b2"):
                def fvh(e, _e=ec, _pre=cnt["V"]):
                    W(e, "T", t_h[_e])
                    if _e == 0:
                        for wv in evw:
                            W(e, *wv)
                    i = e.tensor_copy(T[dst][:, _e * FW:_e * FW + 1024],
                                      psH(_e % 2, 0, 1024))
                    AW(i, "V", _pre)
                    i = e.tensor_copy(T[dst][:, _e * FW + 1024:(_e + 1) * FW],
                                      psH(_e % 2, 1024, 2048))
                    I(i, "V", 1)
                pm.em("vector", fvh)
                v_he[ec] = bump("V")
            else:
                def fsh(e, _e=ec, _s0=cnt["SC"]):
                    W(e, "T", t_h[_e])
                    if _e == 0:
                        for wv in sw0:
                            W(e, *wv)
                    s = _s0
                    for sp in range(4):
                        i = e.activation(tt512, psH(_e % 2, 512 * sp, 512 * (sp + 1)),
                                         AF.Tanh, bias=T["fbfm"][:, _e:_e + 1],
                                         scale=1.0)
                        AW(i, "SC", s); I(i, "SC", 1); s += 1
                        i = e.activation(
                            T[dst][:, _e * FW + 512 * sp: _e * FW + 512 * (sp + 1)],
                            tt512, AF.Silu)
                        AW(i, "SC", s); I(i, "SC", 1); s += 1
                pm.em("scalar", fsh)
                sc_free[ec] = bump("SC", 8)
            if wload is not None and ec in wcB_sched:
                gidx = wcB_sched[ec]
                if gidx < len(wload):
                    emit_wcast(wload[gidx], pm)
        return t_h, v_he, sc_free

    def emit_fvK():
      def fvK(e, _pre=cnt["V"]):
        W(e, *g_tot)
        inst = e.tensor_tensor(T["Kfm"][:, :], T["mw0fm"][:, :],
                               T["totfm"][:, :], op=ALU.mult)
        AW(inst, "V", _pre)
        I(inst, "V", 1)
        inst = e.tensor_tensor(T["Kfm"][:, :], T["Kfm"][:, :], KBv[:, :],
                               op=ALU.add)
        AW(inst, "V", _pre + 1)
        I(inst, "V", 1)
      pm.em("vector", fvK)
      bump("V", 2)

    v_kadd = {}

    def emit_kadd(rng):
      for ec in rng:
        def fka(e, _e=ec, _pre=cnt["V"]):
            if _e == 0:
                W(e, "T", t_f[NB - 1])
            i = e.tensor_scalar(T["P4"][:, _e * FW:(_e + 1) * FW],
                                T["P4"][:, _e * FW:(_e + 1) * FW],
                                T["Kfm"][:, _e:_e + 1], None, op0=ALU.add)
            AW(i, "V", max(_pre, v_fe.get(NB - 1, 0)))
            I(i, "V", 1)
        pm.em("vector", fka)
        v_kadd[ec] = bump("V")


    t_h1, v_he1, _ = dense_pass(
        "h1", "WB1", "xn", XNW, 128, "P2", "copy",
        tw0=[("SC", sc_p[NB - 1]), ("V", v_pk[NB - 1]), ("V", v_wcG[3])],
        tw1=[("V", v_fe[NB - 1])],
        wload=[8, 9, 10, 11])
    t_h2, v_he2, _ = dense_pass(
        "h2", "WB2", "P2", FW, 0, "xn", "copy",
        tw0=[("V", v_he1[CH - 1]), ("V", v_wcG[7])],
        evw=[("T", t_h1[CH - 1])],
        wload=[12, 13, 14, 15])
    emit_fvK()
    emit_kadd(range(0, 8))
    def f_xcopy(e):
        I(e.dma_start(out=ap["out"], in_=ap["x"][128:2176, :]), "B1R", 16)
    pm.em("gpsimd", f_xcopy)
    g_xcopy = bump("B1R", 16)

    def fgrow(e):
        W(e, "V", v_pk[NB - 1])
        i = e.dma_start(out=b2bc.rearrange("p (o n) -> p o n", o=1),
                        in_=ap["fb"].rearrange("(o n) -> o n", o=1)
                        .partition_broadcast(128))
        I(i, "GW1", 16)
    pm.em("gpsimd", fgrow)
    uev["g_fb"] = ("GW1", bump("GW1", 16))

    def fgrow2(e):
        W(e, "V", uev["v_add_h3"][NTO - 1])
        i = e.dma_start(out=b2bc.rearrange("p (o n) -> p o n", o=1),
                        in_=ap["b2"].rearrange("(o n) -> o n", o=1)
                        .partition_broadcast(128))
        I(i, "GW1", 16)
    pm.em("gpsimd", fgrow2)
    uev["g_b2"] = ("GW1", bump("GW1", 16))
    t_h3, sc_g = dense_tm(
        "h3", "WB0", "xn", FW, "P3", "tanh_silu",
        tw0=[("V", v_he2[CH - 1]), ("V", v_wcG[11])],
        sw0=[("V", v_wcG[15])])


    v_prod, s_store = {}, {}

    def emit_E_tile(k):
        def fvpr(e, _k=k, _pre=cnt["V"]):
            W(e, "SC", sc_g[_k])
            if _k == 0:
                W(e, "V", v_wcG[15])
                W(e, "SC", sc_g[NTO - 1])
            if _k >= 2:
                W(e, *s_store[_k - 2])
            i = e.tensor_tensor(SGg[0][:, 1024 * (_k % 2):1024 * (_k % 2) + 1024],
                                T["P2"][:, 1024 * _k:1024 * (_k + 1)],
                                T["P3"][:, 1024 * _k:1024 * (_k + 1)], op=ALU.mult)
            AW(i, "V", _pre)
            I(i, "V", 1)
        pm.em("vector", fvpr)
        v_prod[k] = bump("V")

        def fss(e, _k=k):
            W(e, "V", v_prod[_k])
            if _k <= 1:
                W(e, "B1R", g_xcopy)
            I(e.dma_start(out=ap["out"][128 * _k:128 * (_k + 1), :],
                          in_=SGg[0][:, 1024 * (_k % 2):1024 * (_k % 2) + 1024],
                          accum_op=ALU.add), f"SO{_k % 2}", 16)
        pm.em("gpsimd", fss)
        s_store[k] = (f"SO{k % 2}", bump(f"SO{k % 2}", 16))

    t_hs, v_hes = dense_tm(
        "s", "WB1", "P4", FW, "P2", "copy",
        tw0=[("V", v_kadd[CH - 1]), ("V", v_wcG[15]),
             ("V", uev["v_add_h3"][NTO - 2])],
        tw1=[("V", uev["v_add_h3"][NTO - 1])],
        per_tile=emit_E_tile, badd="g_b2")

    so0_final, so1_final = cnt["SO0"], cnt["SO1"]

    def fend(e):
        W(e, "SO0", so0_final)
        W(e, "SO1", so1_final)
    pm.em("sync", fend)
    run_phase(pm)
    esM.close()

    es.close()
    return nc


_NC_CACHE = None


def kernel(L0_attn_input_data, L0_attn_W1, L0_attn_bias1, L0_attn_W2, L0_attn_bias2,
           L0_attn_memory_weights, L0_ffn_weight, L0_ffn_aux_weight1,
           L0_ffn_aux_weight2, L0_ffn_bias):
    global _NC_CACHE
    x = np.ascontiguousarray(np.asarray(L0_attn_input_data, dtype=np.float32)[0])
    tabs = _make_tables()
    cst1 = np.zeros((128, 1419), dtype=np.float32)
    cst1[:, 0:128] = np.eye(128, dtype=np.float32)
    cst1[0, 128:256] = 1.0
    off = 256
    for nm in ("tah", "tal", "te1h", "te1l", "te2h", "te2l", "tb1h", "tb1l",
               "tb2h", "tb2l"):
        t = tabs[nm]
        cst1[:, off:off + t.shape[1]] = t
        off += t.shape[1]
    for u in range(1, 41):
        cst1[u, 1380:1380 + min(u - 1, 39)] = 1.0
    mw = np.asarray(L0_attn_memory_weights, dtype=np.float32)
    b1 = np.asarray(L0_attn_bias1, dtype=np.float32)
    b2 = np.asarray(L0_attn_bias2, dtype=np.float32)
    fb = np.asarray(L0_ffn_bias, dtype=np.float32)
    cst2 = np.zeros((128, 2081), dtype=np.float32)
    cst2[0:41, 0:1024] = mw
    cst2[0:41, 1024:2048] = mw[0]
    cst2[:, 2048:2056] = b1.reshape(8, 128).T
    cst2[:, 2056:2064] = b2.reshape(8, 128).T
    cst2[:, 2064:2072] = fb.reshape(8, 128).T
    cst2[:, 2072:2080] = mw[0].reshape(8, 128).T
    cst2[:, 2080] = 1.0
    shared = {
        "W1": np.ascontiguousarray(L0_attn_W1, dtype=np.float32),
        "W2": np.ascontiguousarray(L0_attn_W2, dtype=np.float32),
        "Wf": np.ascontiguousarray(L0_ffn_weight, dtype=np.float32),
        "A1": np.ascontiguousarray(L0_ffn_aux_weight1, dtype=np.float32),
        "A2": np.ascontiguousarray(L0_ffn_aux_weight2, dtype=np.float32),
        "b1": b1, "b2": b2, "fb": fb,
        "cst1": cst1,
    }
    in_maps = []
    for c in range(8):
        lo = c * 2048 - 128
        xs = np.zeros((2176, D), dtype=np.float32)
        src_lo = max(lo, 0)
        xs[src_lo - lo:] = x[src_lo:c * 2048 + 2048]
        c2 = cst2.copy()
        c2[:, 2080] = 1.0 if c == 0 else 0.0
        im = dict(shared)
        im["x"] = xs
        im["cst2"] = np.ascontiguousarray(c2)
        in_maps.append(im)
    if _NC_CACHE is None:
        _NC_CACHE = build_nc()
    res = run_bass_kernel_spmd(_NC_CACHE, in_maps, core_ids=list(range(8)))
    out = np.concatenate([res.results[c]["out"] for c in range(8)], axis=0)
    return out[None].astype(np.float32)
